# version 01
# baseline (speedup 1.0000x reference)
import contextlib
import numpy as np
import concourse.bass as bass
import concourse.mybir as mybir
from concourse.bass_utils import run_bass_kernel_spmd

F32 = mybir.dt.float32
BF16 = mybir.dt.bfloat16
I32 = mybir.dt.int32
U8 = mybir.dt.uint8
AF = mybir.ActivationFunctionType
ALU = mybir.AluOpType
AX = mybir.AxisListType

S = 4096
DM = 1024
NT = S // 128
FCONST_VALS = (1e-5, 1e-6, 1.0, 1e-9, 1e-30)

ENGS = ("sync", "gpsimd", "tensor", "vector", "scalar")
CMP_SEM = {"sync": "cy", "gpsimd": "cg", "tensor": "ct", "vector": "cv",
           "scalar": "cs"}


class Prog:
    def __init__(self):
        self.flush_ap = {}
        self.ch_queue = {}
        self.stream = {e: [] for e in ENGS}
        self.cnt = {}
        for s in CMP_SEM.values():
            self.cnt[s] = 0
        self.waited = {e: {} for e in ENGS}

    def _sem(self, s):
        if s not in self.cnt:
            self.cnt[s] = 0

    def _waits(self, engine, deps):
        import os
        strict = bool(os.environ.get("SIM_STRICT"))
        mx = {}
        own = CMP_SEM[engine]
        wt = self.waited[engine]
        for d in deps:
            if d is None:
                continue
            s, c = d
            if c <= 0 or (s == own and not strict):
                continue
            if c > mx.get(s, 0):
                mx[s] = c
        w = []
        for s, c in mx.items():
            if wt.get(s, 0) >= c:
                continue
            wt[s] = c
            w.append((s, c))
        return w

    def op(self, engine, fn, deps=(), n=1, flush=False):
        import os
        if os.environ.get("SIM_STRICT"):
            deps = list(deps) + [(CMP_SEM[engine],
                                  self.cnt[CMP_SEM[engine]])]
        """flush=True: append a same-engine dummy wide write after fn and put
        the semaphore increment on it. Needed when fn's output is small
        (<~512B/partition): small writebacks are not reliably visible to
        other engines even after this op's own semaphore increment."""
        w = self._waits(engine, deps)
        sem = CMP_SEM[engine]
        self.cnt[sem] += n
        tok = (sem, self.cnt[sem])
        fap = self.flush_ap.get(engine) if flush else None

        def run(eng, sems, w=w, fn=fn, sem=sem, n=n, fap=fap):
            for (s, c) in w:
                eng.wait_ge(sems[s], c)
            inst = fn(eng)
            if fap is not None:
                inst = eng.memzero(fap)
            inst.then_inc(sems[sem], n)
        self.stream[engine].append(run)
        return tok

    def dma(self, engine, fn, deps=(), ch="misc"):
        ch = "dq_" + ch
        self._sem(ch)
        q = self.ch_queue.setdefault(ch, engine)
        assert q == engine, f"channel {ch} used from both {q} and {engine}"
        w = self._waits(engine, deps)
        self.cnt[ch] += 16
        tok = (ch, self.cnt[ch])

        def run(eng, sems, w=w, fn=fn, ch=ch):
            for (s, c) in w:
                eng.wait_ge(sems[s], c)
            fn(eng).then_inc(sems[ch], 16)
        self.stream[engine].append(run)
        return tok

    def route(self, engine, arms, deps=(), ch="route"):
        ch = "dq_" + ch
        self._sem(ch)
        w = self._waits(engine, deps)
        self.cnt[ch] += 16
        tok = (ch, self.cnt[ch])

        def run(eng, sems, w=w, arms=arms, ch=ch):
            for (s, c) in w:
                eng.wait_ge(sems[s], c)
            for flag_ap, dma_fn in arms:
                v = eng.value_load(flag_ap)
                r = eng.to_reg(v)
                with eng.If_ne(r, 0):
                    dma_fn(eng).then_inc(sems[ch], 16)
                try:
                    eng.free_register(r)
                except Exception:
                    pass
        self.stream[engine].append(run)
        return tok

    def barrier(self):
        toks = [(s, c) for s, c in self.cnt.items()]
        for e in ENGS:
            w = self._waits(e, toks)
            if not w:
                continue

            def run(eng, sems, w=w):
                for (s, c) in w:
                    eng.wait_ge(sems[s], c)
            self.stream[e].append(run)

    def finalize(self, nc, es):
        sems = {s: es.enter_context(nc.semaphore(s)) for s in self.cnt}
        block = es.enter_context(nc.Block())
        for name in ENGS:
            closures = self.stream[name]

            def section(eng, closures=closures):
                for fn in closures:
                    fn(eng, sems)
            getattr(block, name)(section)


class PsumPool:
    def __init__(self, nc, es):
        self.banks = []
        for i in range(6):
            t = es.enter_context(nc.psum_tensor(f"pf{i}", [128, 512], F32))
            self.banks.append({"t": t, "dtype": F32, "free_tok": None})
        for i in range(2):
            t = es.enter_context(nc.psum_tensor(f"pb{i}", [128, 512], BF16))
            self.banks.append({"t": t, "dtype": BF16, "free_tok": None})
        self.rrf = 0
        self.rrb = 0

    def acquire(self, dtype=F32):
        idxs = range(6) if dtype == F32 else range(6, 8)
        for _ in range(len(idxs)):
            if dtype == F32:
                i = idxs[self.rrf % len(idxs)]
                self.rrf += 1
            else:
                i = idxs[self.rrb % len(idxs)]
                self.rrb += 1
            b = self.banks[i]
            if not b.get("in_use"):
                b["in_use"] = True
                return b, list(b.get("free_tok") or [])
        raise RuntimeError("no free PSUM bank")

    @staticmethod
    def release(bank, tok):
        bank["free_tok"] = list(tok) if isinstance(tok, list) else [tok]
        bank["in_use"] = False


class Ctx:
    def __init__(self, nc, es):
        self.nc = nc
        self.es = es
        self.p = Prog()
        self.pool = PsumPool(nc, es)
        self._sb = {}
        self.last = {}
        for e in ("vector", "scalar", "gpsimd"):
            t = es.enter_context(nc.sbuf_tensor(f"flush_{e}", [128, 64], F32))
            self.p.flush_ap[e] = t[:, :]

    def sb(self, name, shape, dtype):
        if name not in self._sb:
            self._sb[name] = self.es.enter_context(
                self.nc.sbuf_tensor(name, shape, dtype))
        else:
            t = self._sb[name]
            assert list(t.shape) == list(shape), f"slot {name} shape mismatch"
        return self._sb[name]


def mm_accum(ctx, psum_ap, pairs, deps, bank_deps=(), start=True, stop=True):
    def fn(eng):
        last = None
        n = len(pairs)
        for i, (lt, rh) in enumerate(pairs):
            last = eng.matmul(psum_ap, lt, rh,
                              start=(start and i == 0),
                              stop=(stop and i == n - 1))
        return last
    return ctx.p.op("tensor", fn, deps=list(deps) + list(bank_deps))


def tr128(ctx, dst_sb_ap, src_sb_ap, consts, deps, dtype=BF16,
          evac_engine="vector", src_base=0):
    p = ctx.p
    bank, bdeps = ctx.pool.acquire(dtype)
    pt = bank["t"]
    m = src_sb_ap.shape[-1]
    pp = src_sb_ap.shape[0]
    identf = consts["ident_bf"] if dtype == BF16 else consts["ident_f32"]
    ident = identf[src_base:src_base + pp, src_base:src_base + pp]
    tmm = p.op("tensor",
               lambda eng: eng.transpose(pt[0:m, 0:pp], src_sb_ap, ident),
               deps=list(deps) + bdeps)
    tev = p.op(evac_engine,
               lambda eng: eng.tensor_copy(dst_sb_ap, pt[0:m, 0:pp]),
               deps=[tmm])
    ctx.pool.release(bank, tev)
    return tev


def emit_ln_stats(ctx, xio, pref, stash, nm_batch, deps0=()):
    p = ctx.p
    xb = [ctx.sb(pref + f"_sxb{i}", [128, DM], F32) for i in range(2)]
    sq = ctx.sb(pref + "_ssq", [128, DM], BF16)
    sums = ctx.sb(pref + "_sums", [128, 32], F32)
    ssq = ctx.sb(pref + "_ssqa", [128, 32], F32)
    mean = ctx.sb(pref + "_mean", [128, 32], F32)
    msqv = ctx.sb(pref + "_msqv", [128, 32], F32)
    ssqn = ctx.sb(pref + "_ssqn", [128, 32], F32)
    var = ctx.sb(pref + "_var", [128, 32], F32)
    lnv = ctx.sb(pref + "_lnv", [128, 32], F32)
    rstd_b = stash[:, 0:32]
    nmr_b = stash[:, 32:64]
    rtoks = []
    prevv = [list(deps0), list(deps0)]
    for tc in range(NT):
        b = tc % 2
        xld = p.dma("gpsimd", lambda eng, tc=tc, b=b: eng.dma_start(
            out=xb[b][:, :], in_=xio["src"][tc]),
            deps=prevv[b], ch=pref + f"_lnx{b}")
        t1 = p.op("vector", lambda eng, tc=tc, b=b: eng.reduce_sum(
            sums[:, tc:tc + 1], xb[b][:, :], axis=AX.X), deps=[xld])
        t2 = p.op("scalar", lambda eng, tc=tc, b=b: eng.activation(
            sq[:, :], xb[b][:, :], AF.Square,
            accum_out=ssq[:, tc:tc + 1]), deps=[xld])
        rtoks += [t1, t2]
        prevv[b] = [t1, t2]
    ta = p.op("scalar", lambda eng: eng.activation(
        mean[:, :], sums[:, :], AF.Copy, scale=1.0 / DM), deps=rtoks, flush=True)
    tb = p.op("scalar", lambda eng: eng.activation(
        msqv[:, :], mean[:, :], AF.Square), deps=[ta], flush=True)
    tc_ = p.op("scalar", lambda eng: eng.activation(
        ssqn[:, :], ssq[:, :], AF.Copy, scale=1.0 / DM), deps=rtoks, flush=True)
    tnm = p.op("vector", lambda eng: eng.tensor_scalar_mul(
        nm_batch[:, :], mean[:, :], -1.0), deps=[ta], flush=True)
    tv1 = p.op("vector", lambda eng: eng.tensor_sub(
        var[:, :], ssqn[:, :], msqv[:, :]), deps=[tb, tc_], flush=True)
    tv2 = p.op("vector", lambda eng: eng.tensor_scalar_max(
        var[:, :], var[:, :], 0.0), deps=[tv1], flush=True)
    t4 = p.op("scalar", lambda eng: eng.activation(
        lnv[:, :], var[:, :], AF.Ln, bias=1e-5), deps=[tv2], flush=True)
    t5 = p.op("scalar", lambda eng: eng.activation(
        rstd_b, lnv[:, :], AF.Exp, scale=-0.5), deps=[t4], flush=True)
    t6 = p.op("vector", lambda eng: eng.tensor_mul(
        nmr_b, nm_batch[:, :], rstd_b), deps=[t5, tnm], flush=True)
    return t6


def ln_apply(ctx, x_ap, z_ap, stash, tc, deps):
    return ctx.p.op("vector", lambda eng: eng.tensor_scalar(
        z_ap, x_ap, stash[:, tc:tc + 1], stash[:, 32 + tc:33 + tc],
        ALU.mult, ALU.add), deps=deps)



def emit_l3_ffn(ctx, xio, hb, consts, xdeps=()):
    nc, p = ctx.nc, ctx.p
    TW = 256
    NW = S // TW
    F1 = 4096

    weff_d = nc.dram_tensor("l3f_weff", [8 * 8 * 128, 512], BF16,
                            kind="Internal").ap()
    k1_d = nc.dram_tensor("l3f_k1", [8 * 32 * 128, 512], BF16,
                          kind="Internal").ap()
    k2_d = nc.dram_tensor("l3f_k2", [2 * 32 * 128, 512], BF16,
                          kind="Internal").ap()
    weff_v = weff_d.rearrange("(o k p) c -> o k p c", o=8, k=8, p=128)
    k1_v = k1_d.rearrange("(o k p) c -> o k p c", o=8, k=32, p=128)
    k2_v = k2_d.rearrange("(o k p) c -> o k p c", o=2, k=32, p=128)

    cu8a = ctx.sb("l3f_cu8a", [128, F1], U8)
    cu8b = ctx.sb("l3f_cu8b", [128, F1], U8)
    h1fm_early = ctx.sb("l3f_h1fm", [128, 32 * 256], BF16)
    cb16a = h1fm_early[:, 0:F1]
    cb16b = h1fm_early[:, F1:2 * F1]
    k0_ap = hb["L3_ffn_k0"]
    k1_ap = hb["L3_ffn_k1"]
    k2_ap = hb["L3_ffn_k2"]

    sto = None
    prevv = None
    for kc in range(8):
        tla = p.dma("sync", lambda eng, kc=kc: eng.dma_start(
            out=cu8a[:, :], in_=k0_ap[kc * 128:(kc + 1) * 128, :]),
            deps=[prevv], ch="cu8a")
        tlb = p.dma("sync", lambda eng, kc=kc: eng.dma_start(
            out=cu8b[:, :], in_=k0_ap[1024 + kc * 128:1024 + (kc + 1) * 128, :]),
            deps=[prevv], ch="cu8b")
        t1 = p.op("vector", lambda eng: eng.tensor_scalar_mul(
            cb16b[:, :], cu8b[:, :], 2.0), deps=[tlb, sto])
        t2 = p.op("vector", lambda eng: eng.scalar_tensor_tensor(
            cb16a[:, :], cu8a[:, :], 2.0, cb16b[:, :], ALU.mult, ALU.subtract),
            deps=[tla, t1])
        prevv = t2
        for ob in range(8):
            sto = p.dma("gpsimd", lambda eng, ob=ob, kc=kc: eng.dma_start(
                out=weff_v[ob, kc, :, :],
                in_=cb16a[:, ob * 512:(ob + 1) * 512]), deps=[t2], ch="stgA")
    for kcg in range(32):
        tla = p.dma("sync", lambda eng, kcg=kcg: eng.dma_start(
            out=cu8a[:, :], in_=k1_ap[kcg * 128:(kcg + 1) * 128, :]),
            deps=[prevv, sto], ch="cu8a")
        t1 = p.op("vector", lambda eng: eng.tensor_scalar(
            cb16a[:, :], cu8a[:, :], 2.0, -1.0, ALU.mult, ALU.add),
            deps=[tla, sto])
        prevv = t1
        for ob in range(8):
            sto = p.dma("gpsimd", lambda eng, ob=ob, kcg=kcg: eng.dma_start(
                out=k1_v[ob, kcg, :, :],
                in_=cb16a[:, ob * 512:(ob + 1) * 512]), deps=[t1], ch="stgA")
    for kcg in range(32):
        tla = p.dma("sync", lambda eng, kcg=kcg: eng.dma_start(
            out=cu8a[:, 0:DM], in_=k2_ap[kcg * 128:(kcg + 1) * 128, :]),
            deps=[prevv, sto], ch="cu8a")
        t1 = p.op("vector", lambda eng: eng.tensor_scalar(
            cb16a[:, 0:DM], cu8a[:, 0:DM], 2.0, -1.0, ALU.mult, ALU.add),
            deps=[tla, sto])
        prevv = t1
        for ob in range(2):
            sto = p.dma("gpsimd", lambda eng, ob=ob, kcg=kcg: eng.dma_start(
                out=k2_v[ob, kcg, :, :],
                in_=cb16a[:, ob * 512:(ob + 1) * 512]), deps=[t1], ch="stgA")
    p.barrier()

    sc = ctx.sb("l3f_sc", [128, 8], F32)
    tso = p.dma("sync", lambda eng: eng.dma_start(
        out=sc[:, 0:3], in_=hb["l3_s012"][:, :]), deps=[], ch="sc")
    tsc = p.op("vector", lambda eng: eng.tensor_scalar_mul(
        sc[:, 4:6], sc[:, 0:2], 2.0), deps=[tso], flush=True)
    s0x2, s1x2, s2c = sc[:, 4:5], sc[:, 5:6], sc[:, 2:3]

    z = ctx.sb("l3f_z", [128, 2 * DM], BF16)
    xb = ctx.sb("l3f_xb", [128, 2 * DM], F32)
    l3stash = ctx.sb("l3f_stash", [128, 64], F32)
    l3nmb = ctx.sb("l3f_nmb", [128, 32], F32)
    tstats = emit_ln_stats(ctx, xio, "l3f", l3stash[:, :], l3nmb)
    zfm = ctx.sb("l3f_zfm", [128, 8 * TW], BF16)
    h1fm = ctx.sb("l3f_h1fm", [128, 32 * TW], BF16)
    h2fm = ctx.sb("l3f_h2fm", [128, 32 * TW], BF16)
    kbuf = [ctx.sb(f"l3f_kb{i}", [128, 8 * 512], BF16) for i in range(2)]
    kbtok = [None, None]

    def tensor_now():
        return ("ct", ctx.p.cnt["ct"])

    def kload(b, view, deps=()):
        kbtok[b] = p.dma(
            "sync" if b == 0 else "gpsimd",
            lambda eng, b=b, view=view: eng.dma_start(
                out=kbuf[b][:, :].rearrange("p (k c) -> p k c", c=512),
                in_=view.rearrange("k p c -> p k c")),
            deps=[kbtok[b], ctx.last.get(f"l3f_kb{b}_rd")] + list(deps),
            ch=f"kb{b}")
        return kbtok[b]

    prev_zreads = None
    xst = [None, None]
    xw_by_ti = {0: [], 1: []}
    for w in range(NW):
        t0 = w * 2
        ztoks = []
        xld = []
        for i in range(2):
            xld.append(p.dma("gpsimd", lambda eng, i=i, t0=t0: eng.dma_start(
                out=xb[:, i * DM:(i + 1) * DM], in_=xio["src"][t0 + i]),
                deps=[xst[i]], ch=f"l3f_xt{i}"))
        for i in range(2):
            ztoks.append(ln_apply(
                ctx, xb[:, i * DM:(i + 1) * DM],
                z[:, i * DM:(i + 1) * DM], l3stash, t0 + i,
                deps=list(xdeps) + [xld[i], tstats, prev_zreads]))
        zfm_toks = []
        for kc in range(8):
            for i in range(2):
                zfm_toks.append(tr128(
                    ctx, zfm[:, kc * TW + i * 128: kc * TW + (i + 1) * 128],
                    z[:, i * DM + kc * 128: i * DM + (kc + 1) * 128],
                    consts, deps=[ztoks[i]]))
        prev_zreads = tensor_now()
        h1_toks = []
        for ob in range(8):
            b = ob % 2
            wt = kload(b, weff_v[ob])
            for j in range(4):
                bank, bdeps = ctx.pool.acquire(F32)
                pairs = [(kbuf[b][:, kc * 512 + j * 128:
                                  kc * 512 + (j + 1) * 128],
                          zfm[:, kc * TW:(kc + 1) * TW]) for kc in range(8)]
                tmm = mm_accum(ctx, bank["t"][:, 0:TW], pairs,
                               deps=zfm_toks + [wt], bank_deps=bdeps)
                cglob = ob * 4 + j
                tev = p.op("scalar", lambda eng, bank=bank, cglob=cglob:
                           eng.activation(h1fm[:, cglob * TW:(cglob + 1) * TW],
                                          bank["t"][:, 0:TW], AF.Relu,
                                          scale=s0x2),
                           deps=[tmm, tsc])
                ctx.pool.release(bank, tev)
                h1_toks.append(tev)
            ctx.last[f"l3f_kb{b}_rd"] = tensor_now()
        h2_toks = []
        for ob2 in range(8):
            bks = [ctx.pool.acquire(F32) for _ in range(4)]
            for sub in range(4):
                b = sub % 2
                wt = kload(b, k1_v[ob2, sub * 8:(sub + 1) * 8])
                for j in range(4):
                    bank, bdeps = bks[j]
                    pairs = [(kbuf[b][:, kk * 512 + j * 128:
                                      kk * 512 + (j + 1) * 128],
                              h1fm[:, (sub * 8 + kk) * TW:
                                   (sub * 8 + kk + 1) * TW])
                             for kk in range(8)]
                    mm_accum(ctx, bank["t"][:, 0:TW], pairs,
                             deps=h1_toks + [wt],
                             bank_deps=bdeps if sub == 0 else [],
                             start=(sub == 0), stop=(sub == 3))
                ctx.last[f"l3f_kb{b}_rd"] = tensor_now()
            for j in range(4):
                bank, _ = bks[j]
                cglob = ob2 * 4 + j
                tmm_tok = tensor_now()
                tev = p.op("scalar", lambda eng, bank=bank, cglob=cglob:
                           eng.activation(
                               h2fm[:, cglob * TW:(cglob + 1) * TW],
                               bank["t"][:, 0:TW], AF.Relu, scale=s1x2),
                           deps=[tmm_tok, tsc])
                ctx.pool.release(bank, tev)
                h2_toks.append(tev)
        for ob3 in range(2):
            bks = [ctx.pool.acquire(F32) for _ in range(2)]
            for sub in range(4):
                b = sub % 2
                wt = kload(b, k2_v[ob3, sub * 8:(sub + 1) * 8])
                for ti in range(2):
                    bank, bdeps = bks[ti]
                    pairs = [(h2fm[:, (sub * 8 + kc) * TW + ti * 128:
                                   (sub * 8 + kc) * TW + (ti + 1) * 128],
                              kbuf[b][:, kc * 512:(kc + 1) * 512])
                             for kc in range(8)]
                    mm_accum(ctx, bank["t"][:, :], pairs,
                             deps=h2_toks + [wt],
                             bank_deps=bdeps if sub == 0 else [],
                             start=(sub == 0), stop=(sub == 3))
                ctx.last[f"l3f_kb{b}_rd"] = tensor_now()
            for ti in range(2):
                bank, _ = bks[ti]
                tmm_tok = tensor_now()
                xsl = xb[:, ti * DM + ob3 * 512: ti * DM + (ob3 + 1) * 512]
                txw = p.op("vector", lambda eng, bank=bank, xsl=xsl:
                           eng.scalar_tensor_tensor(xsl, bank["t"][:, :], s2c,
                                                    xsl, ALU.mult, ALU.add),
                           deps=[tmm_tok, tsc] + ztoks)
                ctx.pool.release(bank, txw)
                xw_by_ti[ti].append(txw)
        for ti in range(2):
            xst[ti] = p.dma("sync", lambda eng, ti=ti, t0=t0: eng.dma_start(
                out=xio["dst"][t0 + ti], in_=xb[:, ti * DM:(ti + 1) * DM]),
                deps=xw_by_ti[ti][-2:], ch=f"l3f_xs{ti}")
        xw_by_ti = {0: [], 1: []}
    return None



BLOCK_EMITTERS = {}
NEEDS_CMASK = {"l1_attn", "l2_ffn"}


def build_blocks_program(block_names, input_specs):
    nc = bass.Bass("TRN2", target_bir_lowering=False)
    hb = {}
    for name, (shape, dt) in input_specs.items():
        hb[name] = nc.dram_tensor(name, list(shape), dt,
                                  kind="ExternalInput").ap()
    x_in = nc.dram_tensor("x_in", [S, DM], F32, kind="ExternalInput").ap()
    x_out = nc.dram_tensor("x_out", [S, DM], F32, kind="ExternalOutput").ap()
    n = len(block_names)
    mids = []
    for i in range(n - 1):
        mids.append(nc.dram_tensor(f"x_mid{i % 2}", [S, DM], F32,
                                   kind="Internal").ap() if i < 2 else
                    mids[i - 2])
    es = contextlib.ExitStack()
    with es:
        ctx = Ctx(nc, es)
        p = ctx.p
        idb = ctx.sb("ident_bf", [128, 128], BF16)
        idf = ctx.sb("ident_f32", [128, 128], F32)
        consts = {"ident_bf": idb[:, :], "ident_f32": idf[:, :]}
        p.dma("gpsimd", lambda eng: eng.dma_start(
            out=idb[:, :], in_=hb["ident"][:, :]), ch="init_g")
        p.dma("sync", lambda eng: eng.dma_start(
            out=idf[:, :], in_=hb["ident"][:, :]), ch="init")
        fc = ctx.sb("fconst_sb", [128, len(FCONST_VALS)], F32)
        p.dma("sync", lambda eng: eng.dma_start(
            out=fc[:, :], in_=hb["fconst"][:, :]), ch="init")
        for j, v in enumerate(FCONST_VALS):
            nc.const_aps.register(F32, v, fc[:, j:j + 1])
        onr = ctx.sb("onesrow_sb", [1, 128], F32)
        p.dma("sync", lambda eng: eng.dma_start(
            out=onr[:, :], in_=hb["onesrow"][:, :]), ch="init")
        consts["ones_f32"] = onr
        onb = ctx.sb("onesbf_sb", [1, 128], BF16)
        p.dma("gpsimd", lambda eng: eng.dma_start(
            out=onb[:, :], in_=hb["onesrow"][:, :]), ch="init_g")
        consts["onesbf"] = onb
        if any(bn in NEEDS_CMASK for bn in block_names):
            cmk = ctx.sb("cmask_sb", [128, 4 * 512], BF16)
            p.dma("gpsimd", lambda eng: eng.dma_start(
                out=cmk[:, :], in_=hb["cmask"][:, :]), ch="init_g")
            consts["cmask"] = cmk
        else:
            consts["cmask"] = None
        p.barrier()
        for i, bn in enumerate(block_names):
            src_ap = x_in if i == 0 else mids[i - 1]
            dst_ap = x_out if i == n - 1 else mids[i]
            xio = {
                "src": src_ap.rearrange("(n p) d -> n p d", p=128),
                "dst": dst_ap.rearrange("(n p) d -> n p d", p=128),
            }
            BLOCK_EMITTERS[bn](ctx, xio, hb, consts)
            p.barrier()
        p.finalize(nc, es)
    return nc


def _emit_l3_ffn_wrap(ctx, xio, hb, consts):
    return emit_l3_ffn(ctx, xio, hb, consts)


BLOCK_EMITTERS["l3_ffn"] = _emit_l3_ffn_wrap


COMMON_SPECS = {
    "ident": ((128, 128), F32),
    "fconst": ((128, len(FCONST_VALS)), F32),
    "onesrow": ((1, 128), F32),
    "cmask": ((128, 4 * 512), F32),
}


def common_arrays():
    jp = np.arange(128)[:, None]
    tq = np.arange(512)[None, :]
    cm = np.concatenate([(jp <= tq - 128 * r).astype(np.float32)
                         for r in range(4)], axis=1)
    return {
        "ident": np.eye(128, dtype=np.float32),
        "fconst": np.ascontiguousarray(
            np.broadcast_to(np.asarray(FCONST_VALS, np.float32),
                            (128, len(FCONST_VALS)))),
        "onesrow": np.ones((1, 128), np.float32),
        "cmask": np.ascontiguousarray(cm),
    }


def prep_l3_ffn(inputs):
    specs = {
        "L3_ffn_k0": ((2048, 4096), U8),
        "L3_ffn_k1": ((4096, 4096), U8),
        "L3_ffn_k2": ((4096, 1024), U8),
        "l3_s012": ((128, 3), F32),
    }
    specs.update(COMMON_SPECS)
    s012 = np.stack([np.broadcast_to(inputs["L3_ffn_s0"], (128,)),
                     np.broadcast_to(inputs["L3_ffn_s1"], (128,)),
                     np.broadcast_to(inputs["L3_ffn_s2"], (128,))], axis=1)
    arrs = {
        "L3_ffn_k0": np.ascontiguousarray(inputs["L3_ffn_k0"].view(np.uint8)),
        "L3_ffn_k1": np.ascontiguousarray(inputs["L3_ffn_k1"].view(np.uint8)),
        "L3_ffn_k2": np.ascontiguousarray(inputs["L3_ffn_k2"].view(np.uint8)),
        "l3_s012": np.ascontiguousarray(s012, dtype=np.float32),
    }
    arrs.update(common_arrays())
    return specs, arrs





def attn_pairs(ctx, hb, consts, cfg, zfm, att, ztr):
    nc, p = ctx.nc, ctx.p
    pref = cfg["pref"]
    scale = cfg["scale"]
    causal = cfg["causal"]
    npairs = cfg.get("npairs", 8)
    pool = ctx.pool

    def tensor_now():
        return ("ct", p.cnt["ct"])

    qkwo = ctx.sb(pref + "_qkwo", [128, 2 * S], BF16)
    qfm = qkwo[:, 0:S]
    kfm = qkwo[:, S:2 * S]
    vfm = ctx.sb(pref + "_vfm", [128, S], BF16)
    vaug = [ctx.sb(pref + f"_va{h}", [128, 32 * 65], BF16) for h in range(2)]
    wp = [ctx.sb(pref + f"_wp{j}", [128, 8 * 128], BF16) for j in range(3)]
    ptile = [ctx.sb(pref + f"_pt{j}", [128, 512], BF16) for j in range(2)]
    nsb = ctx.sb(pref + "_nsb", [128, 512], F32)
    rrow = ctx.sb(pref + "_rr", [1, 512], F32)
    cmk = consts["cmask"]
    ones = consts["ones_f32"]

    vone = []
    for h in range(2):
        ap = vaug[h][:, :].rearrange("p (n c) -> p n c", c=65)[:, :, 64:65]
        vone.append(p.op("vector", lambda eng, ap=ap: eng.memset(ap, 1.0),
                         deps=[], flush=True))

    for hp in range(npairs):
        wtoks = []
        for j, wkey in enumerate((cfg["wq"], cfg["wk"], cfg["wv"])):
            wtoks.append(p.dma(
                "gpsimd", lambda eng, j=j, wkey=wkey, hp=hp: eng.dma_start(
                    out=wp[j][:, :].rearrange("p (k c) -> p k c", c=128),
                    in_=hb[wkey][hp].rearrange("(k p) c -> p k c", p=128)),
                deps=[ctx.last.get(pref + f"_wp{j}_rd")], ch=pref + f"_wp{j}"))
        proj_toks = []
        for j, dstbuf in enumerate((qfm, kfm, vfm)):
            bkey = ("qb", "kb", "vb")[j]
            for tsl in range(8):
                bank, bdeps = pool.acquire(F32)
                pairs = [(wp[j][:, kc * 128:(kc + 1) * 128],
                          zfm[:, kc * S + tsl * 512: kc * S + tsl * 512 + 512])
                         for kc in range(8)]
                tmm = mm_accum(ctx, bank["t"][:, :], pairs,
                               deps=ztr + [wtoks[j]], bank_deps=bdeps)
                if cfg.get(bkey) is not None:
                    bap = cfg[bkey][:, hp:hp + 1]
                    tev = p.op("scalar",
                               lambda eng, bank=bank, dstbuf=dstbuf, tsl=tsl,
                               bap=bap:
                               eng.activation(
                                   dstbuf[:, tsl * 512:(tsl + 1) * 512],
                                   bank["t"][:, :], AF.Identity, bias=bap),
                               deps=[tmm, cfg.get(bkey + "_tok")])
                else:
                    tev = p.op("scalar",
                               lambda eng, bank=bank, dstbuf=dstbuf, tsl=tsl:
                               eng.copy(
                                   dstbuf[:, tsl * 512:(tsl + 1) * 512],
                                   bank["t"][:, :]),
                               deps=[tmm])
                pool.release(bank, tev)
                proj_toks.append(tev)
        ctx.last[pref + "_wp0_rd"] = tensor_now()
        ctx.last[pref + "_wp1_rd"] = tensor_now()
        ctx.last[pref + "_wp2_rd"] = tensor_now()
        vtr = []
        for h in range(2):
            for tc in range(NT):
                vtr.append(tr128(
                    ctx, vaug[h][:, tc * 65: tc * 65 + 64],
                    vfm[h * 64:(h + 1) * 64, tc * 128:(tc + 1) * 128],
                    consts, deps=[proj_toks[16 + tsl_of(tc)], vone[h]],
                    src_base=h * 64))
        for h in range(2):
            hb64 = h * 64
            for tsl in range(8):
                pvb, pvdeps = pool.acquire(F32)
                jmax = (4 * tsl + 3) if causal else 31
                for jc in range(jmax + 1):
                    scb, scdeps = pool.acquire(F32)
                    tmm = mm_accum(
                        ctx, scb["t"][:, :],
                        [(kfm[hb64:hb64 + 64, jc * 128:(jc + 1) * 128],
                          qfm[hb64:hb64 + 64, tsl * 512: tsl * 512 + 512])],
                        deps=proj_toks, bank_deps=scdeps)
                    b = jc % 2
                    texp = p.op("scalar",
                                lambda eng, scb=scb, b=b: eng.activation(
                                    ptile[b][:, :], scb["t"][:, :], AF.Exp,
                                    scale=scale),
                                deps=[tmm,
                                      ctx.last.get(pref + f"_pt{b}_rd")])
                    pool.release(scb, texp)
                    tlast = texp
                    if causal and jc >= 4 * tsl:
                        r = jc - 4 * tsl
                        tlast = p.op("vector",
                                     lambda eng, b=b, r=r: eng.tensor_mul(
                                         ptile[b][:, :], ptile[b][:, :],
                                         cmk[:, r * 512:(r + 1) * 512]),
                                     deps=[texp])
                    tpv = mm_accum(
                        ctx, pvb["t"][0:65, :],
                        [(vaug[h][:, jc * 65:(jc + 1) * 65],
                          ptile[b][:, :])],
                        deps=[tlast] + vtr,
                        bank_deps=pvdeps if jc == 0 else [],
                        start=(jc == 0), stop=(jc == jmax))
                    ctx.last[pref + f"_pt{b}_rd"] = tpv
                tev = p.op("scalar", lambda eng, pvb=pvb: eng.copy(
                    nsb[0:65, :], pvb["t"][0:65, :]),
                    deps=[tpv, ctx.last.get(pref + "_nsb_rd")])
                pool.release(pvb, tev)
                trc = p.op("vector", lambda eng: eng.reciprocal(
                    rrow[0:1, :], nsb[64:65, :]), deps=[tev], flush=True)
                bcb, bcdeps = pool.acquire(F32)
                tbc = mm_accum(ctx, bcb["t"][0:64, :],
                               [(ones[0:1, 0:64], rrow[0:1, :])],
                               deps=[trc], bank_deps=bcdeps)
                dst = att[hb64:hb64 + 64, hp * S + tsl * 512:
                          hp * S + tsl * 512 + 512]
                tmu = p.op("vector", lambda eng, dst=dst, bcb=bcb:
                           eng.tensor_mul(dst, nsb[0:64, :],
                                          bcb["t"][0:64, :]),
                           deps=[tbc])
                pool.release(bcb, tmu)
                ctx.last[pref + "_nsb_rd"] = tmu
    att_done = ("cv", p.cnt["cv"])
    qk_free = tensor_now()
    return att_done, qk_free, qkwo


def emit_attn_softmax(ctx, xio, hb, consts, cfg):
    nc, p = ctx.nc, ctx.p
    pref = cfg["pref"]
    scale = cfg["scale"]
    causal = cfg["causal"]
    pool = ctx.pool

    def tensor_now():
        return ("ct", p.cnt["ct"])

    zfm = ctx.sb(pref + "_zfm", [128, 8 * S], BF16)
    stash = ctx.sb(pref + "_stash", [128, 64], F32)
    xbz = ctx.sb(pref + "_xbz", [128, DM], F32)
    ztile = ctx.sb(pref + "_zt", [128, DM], BF16)
    att = ctx.sb(pref + "_att", [128, 8 * S], BF16)
    xbo = ctx.sb(pref + "_xbo", [128, DM], F32)
    osb = ctx.sb(pref + "_osb", [128, DM], F32)

    nmb = ctx.sb(pref + "_nmb", [128, 32], F32)
    tstats = emit_ln_stats(ctx, xio, pref, stash[:, :], nmb)
    prev_zrd = None
    ztr = []
    for tc in range(NT):
        xld = p.dma("gpsimd", lambda eng, tc=tc: eng.dma_start(
            out=xbz[:, :], in_=xio["src"][tc]),
            deps=[prev_zrd, ctx.last.get(pref + "_zap"), tstats],
            ch=pref + "_xz")
        zt = ln_apply(ctx, xbz[:, :], ztile[:, :], stash, tc,
                      deps=[xld, tstats, prev_zrd])
        ctx.last[pref + "_zap"] = zt
        for kc in range(8):
            ztr.append(tr128(
                ctx, zfm[:, kc * S + tc * 128: kc * S + (tc + 1) * 128],
                ztile[:, kc * 128:(kc + 1) * 128], consts, deps=[zt]))
        prev_zrd = tensor_now()

    att_done, qk_free, qkwo = attn_pairs(ctx, hb, consts, cfg, zfm, att, ztr)
    wo_sb = qkwo[:, :]
    wo_toks = []
    for kc in range(8):
        wo_toks.append(p.dma("gpsimd", lambda eng, kc=kc: eng.dma_start(
            out=wo_sb[:, kc * DM:(kc + 1) * DM],
            in_=hb[cfg["wo"]][kc * 128:(kc + 1) * 128, :]),
            deps=[qk_free], ch=pref + "_wo"))

    if cfg.get("dbg_att"):
        dbg = ctx.nc.dram_tensor("dbg_att", [128, 8 * S], F32,
                                 kind="ExternalOutput").ap()
        for kc in range(8):
            p.dma("gpsimd", lambda eng, kc=kc: eng.dma_start(
                out=dbg[:, kc * S:(kc + 1) * S],
                in_=att[:, kc * S:(kc + 1) * S]),
                deps=[att_done], ch="dbg")
        dbgs = ctx.nc.dram_tensor("dbg_stash", [128, 64], F32,
                                  kind="ExternalOutput").ap()
        p.dma("sync", lambda eng: eng.dma_start(
            out=dbgs[:, :], in_=stash[:, :]), deps=[att_done], ch="dbg_s")
        dbgz = ctx.nc.dram_tensor("dbg_zfm", [128, 8 * S], F32,
                                  kind="ExternalOutput").ap()
        for kc in range(8):
            p.dma("gpsimd", lambda eng, kc=kc: eng.dma_start(
                out=dbgz[:, kc * S:(kc + 1) * S],
                in_=zfm[:, kc * S:(kc + 1) * S]),
                deps=[att_done], ch="dbg")

    prev_ost = None
    for tc in range(NT):
        xld = p.dma("gpsimd", lambda eng, tc=tc: eng.dma_start(
            out=xbo[:, :], in_=xio["src"][tc]),
            deps=[ctx.last.get(pref + "_xbo_rd")], ch=pref + "_xo")
        ptoks = []
        for osl in range(2):
            bank, bdeps = pool.acquire(F32)
            pairs = [(att[:, kc * S + tc * 128: kc * S + (tc + 1) * 128],
                      wo_sb[:, kc * DM + osl * 512: kc * DM + osl * 512 + 512])
                     for kc in range(8)]
            tmm = mm_accum(ctx, bank["t"][:, :], pairs,
                           deps=[att_done] + wo_toks, bank_deps=bdeps)
            osl_ap = osb[:, osl * 512:(osl + 1) * 512]
            xsl_ap = xbo[:, osl * 512:(osl + 1) * 512]
            tv = p.op("vector", lambda eng, bank=bank, o=osl_ap, xs=xsl_ap:
                      eng.tensor_add(o, bank["t"][:, :], xs),
                      deps=[tmm, xld, prev_ost])
            pool.release(bank, tv)
            if cfg.get("add_z"):
                rstd = stash[:, tc:tc + 1]
                nmr = stash[:, 32 + tc:33 + tc]
                tz = p.op("vector", lambda eng, o=osl_ap, xs=xsl_ap, rstd=rstd,
                          nmr=nmr: eng.scalar_tensor_tensor(
                              o, xs, rstd, o, ALU.mult, ALU.add),
                          deps=[tv])
                tz2 = p.op("vector", lambda eng, o=osl_ap, nmr=nmr:
                           eng.tensor_scalar_add(o, o, nmr), deps=[tz])
                tv = tz2
            if cfg.get("bo") is not None:
                bo = cfg["bo"]
                tb = p.op("vector", lambda eng, o=osl_ap, bo=bo, osl=osl:
                          eng.tensor_add(o, o, bo[:, osl * 512:(osl + 1) * 512]),
                          deps=[tv])
                tv = tb
            ptoks.append(tv)
        ctx.last[pref + "_xbo_rd"] = ptoks[-1]
        prev_ost = p.dma("sync", lambda eng, tc=tc: eng.dma_start(
            out=xio["dst"][tc], in_=osb[:, :]),
            deps=ptoks, ch=pref + "_xso")
    return


def tsl_of(tc):
    return tc // 4


def bdep_key(bkey):
    return bkey + "_dep"


def emit_l1_attn(ctx, xio, hb, consts):
    emit_attn_softmax(ctx, xio, hb, consts, dict(
        pref="l1a", wq="L1a_wq", wk="L1a_wk", wv="L1a_wv", wo="L1a_wo",
        scale=1.0 / 32.0, causal=True, add_z=False))


BLOCK_EMITTERS["l1_attn"] = emit_l1_attn


def _pair_tile(w):
    return np.ascontiguousarray(
        w.reshape(1024, 8, 128).transpose(1, 0, 2).astype(np.float32))


def prep_l1_attn(inputs):
    W = np.asarray(inputs["L1_attn_Wqvk"], np.float32)
    specs = {
        "L1a_wq": ((8, 1024, 128), F32),
        "L1a_wk": ((8, 1024, 128), F32),
        "L1a_wv": ((8, 1024, 128), F32),
        "L1a_wo": ((1024, 1024), F32),
    }
    specs.update(COMMON_SPECS)
    arrs = {
        "L1a_wq": _pair_tile(W[:, 0:1024]),
        "L1a_wv": _pair_tile(W[:, 1024:2048]),
        "L1a_wk": _pair_tile(W[:, 2048:3072]),
        "L1a_wo": np.ascontiguousarray(np.asarray(inputs["L1_attn_Wo"],
                                                  np.float32)),
    }
    arrs.update(common_arrays())
    return specs, arrs




def emit_l2_attn(ctx, xio, hb, consts):
    p = ctx.p
    bo = ctx.sb("l2a_bo", [128, DM], F32)
    tbo = p.dma("sync", lambda eng: eng.dma_start(
        out=bo[:, :], in_=hb["L2a_bo"][:, :]), ch="l2a_bo")
    import os
    emit_attn_softmax(ctx, xio, hb, consts, dict(
        pref="l2a", wq="L2a_wq", wk="L2a_wk", wv="L2a_wv", wo="L2a_wo",
        scale=1.0 / 32.0, causal=False, add_z=True, bo=bo, bo_tok=tbo,
        dbg_att=bool(os.environ.get("DBG_ATT"))))


BLOCK_EMITTERS["l2_attn"] = emit_l2_attn


def prep_l2_attn(inputs):
    WT = np.ascontiguousarray(np.asarray(inputs["L2_attn_W_qkv"],
                                         np.float32).T).reshape(1024, 16, 192)
    wq = np.ascontiguousarray(WT[:, :, 0:64]).reshape(1024, 1024)
    wk = np.ascontiguousarray(WT[:, :, 64:128]).reshape(1024, 1024)
    wv = np.ascontiguousarray(WT[:, :, 128:192]).reshape(1024, 1024)
    specs = {
        "L2a_wq": ((8, 1024, 128), F32),
        "L2a_wk": ((8, 1024, 128), F32),
        "L2a_wv": ((8, 1024, 128), F32),
        "L2a_wo": ((1024, 1024), F32),
        "L2a_bo": ((128, 1024), F32),
    }
    specs.update(COMMON_SPECS)
    arrs = {
        "L2a_wq": _pair_tile(wq),
        "L2a_wk": _pair_tile(wk),
        "L2a_wv": _pair_tile(wv),
        "L2a_wo": np.ascontiguousarray(
            np.asarray(inputs["L2_attn_W_out"], np.float32).T),
        "L2a_bo": np.ascontiguousarray(np.broadcast_to(
            np.asarray(inputs["L2_attn_b_out"], np.float32), (128, 1024))),
    }
    arrs.update(common_arrays())
    return specs, arrs





def emit_l0_attn(ctx, xio, hb, consts):
    import os
    npairs = int(os.environ.get("L0A_PAIRS", "8"))
    skip_out = bool(os.environ.get("L0A_SKIP_OUT"))
    nc, p = ctx.nc, ctx.p
    pref = "l0a"
    pool = ctx.pool

    def tensor_now():
        return ("ct", p.cnt["ct"])

    zfm = ctx.sb(pref + "_zfm", [128, 8 * S], BF16)
    stash = ctx.sb(pref + "_stash", [128, 64], F32)
    nmb = ctx.sb(pref + "_nmb", [128, 32], F32)
    att = ctx.sb(pref + "_att", [128, 8 * S], BF16)
    qraw = ctx.sb(pref + "_qraw", [128, S], BF16)
    kraw = ctx.sb(pref + "_kraw", [128, S], BF16)
    vfm = ctx.sb(pref + "_vfm", [128, S], BF16)
    sqb = vfm
    vaug = [ctx.sb(pref + f"_va{h}", [128, 32 * 65], BF16) for h in range(2)]
    wp = [ctx.sb(pref + f"_wp{j}", [128, 8 * 128], BF16) for j in range(3)]
    projk = ctx.sb(pref + "_pjk", [128, 32 * 128], BF16)
    projq = kraw
    nrm = ctx.sb(pref + "_nrm", [128, 256], F32)
    qpfm = ctx.sb(pref + "_pjk", [128, 32 * 128], BF16)
    kvb = ctx.sb(pref + "_kvb", [128, 2 * 128], BF16)
    nsb = ctx.sb(pref + "_nsb", [128, 512], F32)
    rrow = ctx.sb(pref + "_rr", [1, 1024], F32)
    rfb = ctx.sb(pref + "_rfb", [128, 130], BF16)
    ztile = ctx.sb(pref + "_zt", [128, DM], BF16)
    osb = ctx.sb(pref + "_osb", [128, DM], F32)
    bo = ctx.sb(pref + "_bo", [128, DM], F32)
    ones = consts["ones_f32"]

    trf = p.dma("gpsimd", lambda eng: eng.dma_start(
        out=rfb[:, :], in_=hb["L0a_rfb"][:, :]), ch=pref + "_rf")
    tbo = p.dma("sync", lambda eng: eng.dma_start(
        out=bo[:, :], in_=hb["L0a_bo"][:, :]), ch=pref + "_bo")

    tstats = emit_ln_stats(ctx, xio, pref, stash[:, :], nmb)
    xbz = ctx.sb(pref + "_sxb0", [128, DM], F32)
    xbo = xbz
    prev_zrd = None
    ztr = []
    for tc in range(NT):
        xld = p.dma("gpsimd", lambda eng, tc=tc: eng.dma_start(
            out=xbz[:, :], in_=xio["src"][tc]),
            deps=[prev_zrd, ctx.last.get(pref + "_zap"), tstats],
            ch=pref + "_xz")
        zt = ln_apply(ctx, xbz[:, :], ztile[:, :], stash, tc,
                      deps=[xld, tstats, prev_zrd])
        ctx.last[pref + "_zap"] = zt
        for kc in range(8):
            ztr.append(tr128(
                ctx, zfm[:, kc * S + tc * 128: kc * S + (tc + 1) * 128],
                ztile[:, kc * 128:(kc + 1) * 128], consts, deps=[zt]))
        prev_zrd = tensor_now()

    vone = []
    for h in range(2):
        ap = vaug[h][:, :].rearrange("p (n c) -> p n c", c=65)[:, :, 64:65]
        vone.append(p.op("vector", lambda eng, ap=ap: eng.memset(ap, 1.0),
                         deps=[], flush=True))

    stages = int(os.environ.get("L0A_STAGES", "255"))
    for hp in range(npairs):
        st_on = (lambda b: True) if hp == 0 else (lambda b: bool(stages & b))
        wtoks = []
        for j, wkey in enumerate(("L0a_wq", "L0a_wk", "L0a_wv")):
            wtoks.append(p.dma(
                "gpsimd", lambda eng, j=j, wkey=wkey, hp=hp: eng.dma_start(
                    out=wp[j][:, :].rearrange("p (k c) -> p k c", c=128),
                    in_=hb[wkey][hp].rearrange("(k p) c -> p k c", p=128)),
                deps=[ctx.last.get(pref + f"_wp{j}_rd")], ch=pref + f"_wp{j}"))
        proj_toks = {0: [], 1: [], 2: []}
        for j, dstbuf in (enumerate((qraw, kraw, vfm)) if st_on(1) else []):
            toks = []
            for tsl in range(8):
                bank, bdeps = pool.acquire(F32)
                pairs = [(wp[j][:, kc * 128:(kc + 1) * 128],
                          zfm[:, kc * S + tsl * 512: kc * S + tsl * 512 + 512])
                         for kc in range(8)]
                tmm = mm_accum(ctx, bank["t"][:, :], pairs,
                               deps=ztr + [wtoks[j],
                                           ctx.last.get(pref + "_sqb_rd")
                                           if j == 2 else None],
                               bank_deps=bdeps)
                tev = p.op("scalar",
                           lambda eng, bank=bank, dstbuf=dstbuf, tsl=tsl:
                           eng.copy(dstbuf[:, tsl * 512:(tsl + 1) * 512],
                                    bank["t"][:, :]),
                           deps=[tmm])
                pool.release(bank, tev)
                toks.append(tev)
            proj_toks[j] = toks
        ctx.last[pref + "_wp0_rd"] = tensor_now()
        ctx.last[pref + "_wp1_rd"] = tensor_now()
        ctx.last[pref + "_wp2_rd"] = tensor_now()
        vtr = []
        for h in (range(2) if st_on(2) else []):
            for tc in range(NT):
                vtr.append(tr128(
                    ctx, vaug[h][:, tc * 65: tc * 65 + 64],
                    vfm[h * 64:(h + 1) * 64, tc * 128:(tc + 1) * 128],
                    consts, deps=[proj_toks[2][tsl_of(tc)], vone[h]],
                    src_base=h * 64))
        vtr_done = tensor_now()
        exp_toks = {0: [], 1: []}
        for j, rawb, projb in (((1, kraw, projk),) if st_on(4) else ()):
            sq_toks = []
            for tsl in range(8):
                sq_toks.append(p.op("scalar", lambda eng, rawb=rawb, tsl=tsl:
                               eng.activation(
                                   sqb[:, tsl * 512:(tsl + 1) * 512],
                                   rawb[:, tsl * 512:(tsl + 1) * 512],
                                   AF.Square),
                               deps=[vtr_done, ctx.last.get(pref + "_sqb_fr")]
                               + proj_toks[j], flush=True))
            pn_toks = []
            for tc in range(NT):
                bank, bdeps = pool.acquire(F32)
                tmm = mm_accum(ctx, bank["t"][:, 0:128],
                               [(rawb[:, tc * 128:(tc + 1) * 128],
                                 rfb[:, 0:128])],
                               deps=proj_toks[j] + [trf], bank_deps=bdeps)
                te1 = p.op("scalar", lambda eng, bank=bank, projb=projb, tc=tc:
                           eng.copy(projb[:, tc * 128:(tc + 1) * 128],
                                    bank["t"][:, 0:128]), deps=[tmm])
                pool.release(bank, te1)
                bank2, bdeps2 = pool.acquire(F32)
                tmm2 = mm_accum(ctx, bank2["t"][:, 0:2],
                                [(sqb[:, tc * 128:(tc + 1) * 128],
                                  rfb[:, 128:130])],
                                deps=sq_toks + [trf], bank_deps=bdeps2)
                te2 = p.op("vector", lambda eng, bank2=bank2, tc=tc, j=j:
                           eng.tensor_copy(nrm[:, j * 64 + tc * 2:
                                               j * 64 + tc * 2 + 2],
                                           bank2["t"][:, 0:2]),
                           deps=[tmm2], flush=True)
                pool.release(bank2, te2)
                pn_toks += [te1, te2]
                ctx.last[pref + "_sqb_fr"] = tensor_now()
            tln = p.op("scalar", lambda eng, j=j: eng.activation(
                nrm[:, 128 + j * 64:128 + j * 64 + 64],
                nrm[:, j * 64:j * 64 + 64], AF.Ln, bias=1e-30),
                deps=pn_toks, flush=True)
            texp = p.op("scalar", lambda eng, j=j: eng.activation(
                nrm[:, 128 + j * 64:128 + j * 64 + 64],
                nrm[:, 128 + j * 64:128 + j * 64 + 64], AF.Exp, scale=0.5),
                deps=[tln], flush=True)
            tneg = p.op("vector", lambda eng, j=j: eng.tensor_scalar_mul(
                nrm[:, 128 + j * 64:128 + j * 64 + 64],
                nrm[:, 128 + j * 64:128 + j * 64 + 64], -1.0),
                deps=[texp], flush=True)
            for tc in range(NT):
                for h in range(2):
                    bcol = nrm[:, 128 + j * 64 + tc * 2 + h:
                               128 + j * 64 + tc * 2 + h + 1]
                    if j == 0:
                        dst = None
                        texp2 = p.op("scalar",
                                     lambda eng, tc=tc, h=h, bcol=bcol:
                                     eng.activation(
                                         projq[:, tc * 128 + h * 64:
                                               tc * 128 + h * 64 + 64],
                                         projq[:, tc * 128 + h * 64:
                                               tc * 128 + h * 64 + 64],
                                         AF.Exp, scale=0.125, bias=bcol),
                                     deps=[tneg])
                    else:
                        texp2 = p.op("scalar",
                                     lambda eng, tc=tc, h=h, bcol=bcol:
                                     eng.activation(
                                         projk[:, tc * 128 + h * 64:
                                               tc * 128 + h * 64 + 64],
                                         projk[:, tc * 128 + h * 64:
                                               tc * 128 + h * 64 + 64],
                                         AF.Exp, scale=0.125, bias=bcol),
                                     deps=[tneg])
                    exp_toks[j].append(texp2)
        kv_toks = []
        for h in (range(2) if st_on(8) else []):
            bank, bdeps = pool.acquire(F32)
            pairs = [(projk[:, tc * 128 + h * 64: tc * 128 + h * 64 + 64],
                      vaug[h][:, tc * 65:(tc + 1) * 65]) for tc in range(NT)]
            tmm = mm_accum(ctx, bank["t"][0:64, 0:65], pairs,
                           deps=exp_toks[1] + [vtr_done] + vtr,
                           bank_deps=bdeps)
            tev = p.op("scalar", lambda eng, bank=bank, h=h: eng.copy(
                kvb[h * 64:(h + 1) * 64, 0:65], bank["t"][0:64, 0:65]),
                deps=[tmm, ctx.last.get(pref + "_kvb_rd")])
            pool.release(bank, tev)
            kv_toks.append(tev)
        kv_mm_done = tensor_now()
        _qsub = int(os.environ.get("L0A_QSUB", "15"))
        _qdst = projk if os.environ.get("L0A_QDST") == "projk" else projq
        for j, rawb, projb in (((0, qraw, _qdst),) if st_on(16) else ()):
            sq_toks = []
            for tsl in (range(8) if _qsub & 1 else []):
                sq_toks.append(p.op("scalar", lambda eng, rawb=rawb, tsl=tsl:
                               eng.activation(
                                   sqb[:, tsl * 512:(tsl + 1) * 512],
                                   rawb[:, tsl * 512:(tsl + 1) * 512],
                                   AF.Square),
                               deps=[vtr_done, ctx.last.get(pref + "_sqb_fr")]
                               + proj_toks[j], flush=True))
            pn_toks = []
            for tc in (range(NT) if _qsub & 2 else []):
                bank, bdeps = pool.acquire(F32)
                tmm = mm_accum(ctx, bank["t"][:, 0:128],
                               [(rawb[:, tc * 128:(tc + 1) * 128],
                                 rfb[:, 0:128])],
                               deps=proj_toks[j] + [trf], bank_deps=bdeps)
                te1 = p.op("scalar", lambda eng, bank=bank, projb=projb, tc=tc:
                           eng.copy(projb[:, tc * 128:(tc + 1) * 128],
                                    bank["t"][:, 0:128]), deps=[tmm])
                pool.release(bank, te1)
                bank2, bdeps2 = pool.acquire(F32)
                tmm2 = mm_accum(ctx, bank2["t"][:, 0:2],
                                [(sqb[:, tc * 128:(tc + 1) * 128],
                                  rfb[:, 128:130])],
                                deps=sq_toks + [trf], bank_deps=bdeps2)
                te2 = p.op("vector", lambda eng, bank2=bank2, tc=tc, j=j:
                           eng.tensor_copy(nrm[:, j * 64 + tc * 2:
                                               j * 64 + tc * 2 + 2],
                                           bank2["t"][:, 0:2]),
                           deps=[tmm2], flush=True)
                pool.release(bank2, te2)
                pn_toks += [te1, te2]
                ctx.last[pref + "_sqb_fr"] = tensor_now()
            if _qsub & 4:
                tln = p.op("scalar", lambda eng, j=j: eng.activation(
                    nrm[:, 128 + j * 64:128 + j * 64 + 64],
                    nrm[:, j * 64:j * 64 + 64], AF.Ln, bias=1e-30),
                    deps=pn_toks, flush=True)
                texp = p.op("scalar", lambda eng, j=j: eng.activation(
                    nrm[:, 128 + j * 64:128 + j * 64 + 64],
                    nrm[:, 128 + j * 64:128 + j * 64 + 64], AF.Exp, scale=0.5),
                    deps=[tln], flush=True)
                tneg = p.op("vector", lambda eng, j=j: eng.tensor_scalar_mul(
                    nrm[:, 128 + j * 64:128 + j * 64 + 64],
                    nrm[:, 128 + j * 64:128 + j * 64 + 64], -1.0),
                    deps=[texp], flush=True)
            for tc in (range(NT) if _qsub & 8 else []):
                for h in range(2):
                    bcol = nrm[:, 128 + j * 64 + tc * 2 + h:
                               128 + j * 64 + tc * 2 + h + 1]
                    texp2 = p.op("scalar",
                                 lambda eng, tc=tc, h=h, bcol=bcol:
                                 eng.activation(
                                     projq[:, tc * 128 + h * 64:
                                           tc * 128 + h * 64 + 64],
                                     projq[:, tc * 128 + h * 64:
                                           tc * 128 + h * 64 + 64],
                                     AF.Exp, scale=0.125, bias=bcol),
                                 deps=[tneg])
                    exp_toks[j].append(texp2)
        qtr = []
        for tc in (range(NT) if st_on(32) else []):
            for h in range(2):
                qtr.append(tr128(
                    ctx, qpfm[h * 64:(h + 1) * 64,
                              tc * 128:(tc + 1) * 128],
                    projq[:, tc * 128 + h * 64: tc * 128 + h * 64 + 64],
                    consts, deps=exp_toks[0] + [kv_mm_done]))
        for h in range(2):
            for tsl in range(8):
                bank, bdeps = pool.acquire(F32)
                tmm = mm_accum(
                    ctx, bank["t"][0:65, :],
                    [(kvb[h * 64:(h + 1) * 64, 0:65],
                      qpfm[h * 64:(h + 1) * 64, tsl * 512: tsl * 512 + 512])],
                    deps=kv_toks + qtr, bank_deps=bdeps)
                tev = p.op("scalar", lambda eng, bank=bank: eng.copy(
                    nsb[0:65, :], bank["t"][0:65, :]),
                    deps=[tmm, ctx.last.get(pref + "_nsb_rd")])
                pool.release(bank, tev)
                tad = p.op("vector", lambda eng: eng.tensor_scalar_add(
                    rrow[0:1, 0:512], nsb[64:65, :], 1e-6),
                    deps=[tev], flush=True)
                trc = p.op("vector", lambda eng: eng.reciprocal(
                    rrow[0:1, 512:1024], rrow[0:1, 0:512]),
                    deps=[tad], flush=True)
                bcb, bcdeps = pool.acquire(F32)
                tbc = mm_accum(ctx, bcb["t"][0:64, :],
                               [(ones[0:1, 0:64], rrow[0:1, 512:1024])],
                               deps=[trc], bank_deps=bcdeps)
                dst = att[h * 64:(h + 1) * 64, hp * S + tsl * 512:
                          hp * S + tsl * 512 + 512]
                tmu = p.op("vector", lambda eng, dst=dst, bcb=bcb:
                           eng.tensor_mul(dst, nsb[0:64, :],
                                          bcb["t"][0:64, :]),
                           deps=[tbc])
                pool.release(bcb, tmu)
                ctx.last[pref + "_nsb_rd"] = tmu
        ctx.last[pref + "_kvb_rd"] = tensor_now()
    att_done = ("cv", p.cnt["cv"])
    qk_free = tensor_now()
    all_done = [qk_free, ("cv", p.cnt["cv"]), ("cs", p.cnt["cs"])]

    wo_sb_t = qraw
    wo_sb2 = kraw
    wo_toks = []
    for kc in range(8):
        dst = wo_sb_t if kc < 4 else wo_sb2
        off = (kc % 4) * DM
        wo_toks.append(p.dma("gpsimd", lambda eng, kc=kc, dst=dst, off=off:
                             eng.dma_start(
                                 out=dst[:, off:off + DM],
                                 in_=hb["L0a_wo"][kc * 128:(kc + 1) * 128, :]),
                             deps=all_done, ch=pref + "_wo"))

    def wo_ap(kc, osl):
        dst = wo_sb_t if kc < 4 else wo_sb2
        off = (kc % 4) * DM
        return dst[:, off + osl * 512: off + osl * 512 + 512]

    prev_ost = None
    for tc in range(NT if not skip_out else 0):
        xld = p.dma("gpsimd", lambda eng, tc=tc: eng.dma_start(
            out=xbo[:, :], in_=xio["src"][tc]),
            deps=[ctx.last.get(pref + "_xbo_rd")], ch=pref + "_xo")
        ptoks = []
        for osl in range(2):
            bank, bdeps = pool.acquire(F32)
            pairs = [(att[:, kc * S + tc * 128: kc * S + (tc + 1) * 128],
                      wo_ap(kc, osl)) for kc in range(8)]
            tmm = mm_accum(ctx, bank["t"][:, :], pairs,
                           deps=[att_done] + wo_toks, bank_deps=bdeps)
            osl_ap = osb[:, osl * 512:(osl + 1) * 512]
            xsl_ap = xbo[:, osl * 512:(osl + 1) * 512]
            tv = p.op("vector", lambda eng, bank=bank, o=osl_ap, xs=xsl_ap:
                      eng.tensor_add(o, bank["t"][:, :], xs),
                      deps=[tmm, xld, prev_ost])
            pool.release(bank, tv)
            tb2 = p.op("vector", lambda eng, o=osl_ap, osl=osl:
                       eng.tensor_add(o, o, bo[:, osl * 512:(osl + 1) * 512]),
                       deps=[tv, tbo])
            ptoks.append(tb2)
        ctx.last[pref + "_xbo_rd"] = ptoks[-1]
        prev_ost = p.dma("sync", lambda eng, tc=tc: eng.dma_start(
            out=xio["dst"][tc], in_=osb[:, :]),
            deps=ptoks, ch=pref + "_xso")
    return


BLOCK_EMITTERS["l0_attn"] = emit_l0_attn


def prep_l0_attn(inputs):
    W = np.ascontiguousarray(np.asarray(inputs["L0_attn_w_qkv"],
                                        np.float32).T)
    rf = np.asarray(inputs["L0_attn_random_features"], np.float32)
    rfb = np.zeros((128, 130), np.float32)
    rfb[0:64, 0:64] = rf.T
    rfb[64:128, 64:128] = rf.T
    rfb[0:64, 128] = 1.0
    rfb[64:128, 129] = 1.0
    specs = {
        "L0a_wq": ((8, 1024, 128), F32),
        "L0a_wk": ((8, 1024, 128), F32),
        "L0a_wv": ((8, 1024, 128), F32),
        "L0a_wo": ((1024, 1024), F32),
        "L0a_bo": ((128, 1024), F32),
        "L0a_rfb": ((128, 130), F32),
    }
    specs.update(COMMON_SPECS)
    arrs = {
        "L0a_wq": _pair_tile(np.ascontiguousarray(W[:, 0:1024])),
        "L0a_wk": _pair_tile(np.ascontiguousarray(W[:, 1024:2048])),
        "L0a_wv": _pair_tile(np.ascontiguousarray(W[:, 2048:3072])),
        "L0a_wo": np.ascontiguousarray(
            np.asarray(inputs["L0_attn_w_out"], np.float32).T),
        "L0a_bo": np.ascontiguousarray(np.broadcast_to(
            np.asarray(inputs["L0_attn_b_out"], np.float32), (128, 1024))),
        "L0a_rfb": rfb,
    }
    arrs.update(common_arrays())
    return specs, arrs





def emit_l2_ffn(ctx, xio, hb, consts):
    nc, p = ctx.nc, ctx.p
    pref = "l2f"
    pool = ctx.pool

    def tensor_now():
        return ("ct", p.cnt["ct"])

    TW = 256
    NW = S // TW
    F1 = 4096

    wfm = ctx.sb(pref + "_zfm", [128, 8 * S], BF16)
    att = ctx.sb(pref + "_att", [128, 8 * S], BF16)
    stash = ctx.sb(pref + "_stash", [128, 64], F32)
    nmb = ctx.sb(pref + "_nmb", [128, 32], F32)
    cols = ctx.sb(pref + "_cols", [128, 128], F32)
    zsq = cols[:, 0:32]
    rrms1 = cols[:, 32:64]
    y2sq = cols[:, 64:96]
    rrms2 = cols[:, 96:128]
    gstash = ctx.sb(pref + "_gst", [128, 32], F32)
    n1bc = ctx.sb(pref + "_n1", [128, DM], F32)
    n2bc = ctx.sb(pref + "_n2", [128, DM], F32)
    gwt = ctx.sb(pref + "_gwt", [128, 64], BF16)
    gsb = ctx.sb(pref + "_gsb", [128, 24], F32)
    y1d = nc.dram_tensor("l2f_y1", [S, DM], F32, kind="Internal").ap()
    y1v = y1d.rearrange("(n p) d -> n p d", p=128)

    tn1 = p.dma("sync", lambda eng: eng.dma_start(
        out=n1bc[:, :], in_=hb["L2f_n1"][:, :]), ch=pref + "_n1")
    tn2 = p.dma("sync", lambda eng: eng.dma_start(
        out=n2bc[:, :], in_=hb["L2f_n2"][:, :]), ch=pref + "_n2")
    tgw = p.dma("gpsimd", lambda eng: eng.dma_start(
        out=gwt[:, :].rearrange("p (k c) -> p k c", c=8),
        in_=hb["L2f_gw"][:, :, :].rearrange("k p c -> p k c")), ch=pref + "_gw")

    tstats = emit_ln_stats(ctx, xio, pref, stash[:, :], nmb)
    xb = ctx.sb(pref + "_sxb0", [128, DM], F32)
    zb = ctx.sb(pref + "_sxb1", [128, DM], F32)
    ztile = ctx.sb(pref + "_zt", [128, DM], BF16)
    sqd = ctx.sb(pref + "_ssq", [128, DM], BF16)
    rs_toks = []
    for tc in range(NT):
        xld = p.dma("gpsimd", lambda eng, tc=tc: eng.dma_start(
            out=xb[:, :], in_=xio["src"][tc]),
            deps=[ctx.last.get(pref + "_xb_rd")], ch=pref + "_x1")
        zt = ln_apply(ctx, xb[:, :], zb[:, :], stash, tc,
                      deps=[xld, tstats, ctx.last.get(pref + "_zb_rd")])
        tsq = p.op("scalar", lambda eng, tc=tc: eng.activation(
            sqd[:, :], zb[:, :], AF.Square, accum_out=zsq[:, tc:tc + 1]),
            deps=[zt])
        ctx.last[pref + "_xb_rd"] = zt
        ctx.last[pref + "_zb_rd"] = tsq
        rs_toks.append(tsq)
    t4 = p.op("scalar", lambda eng: eng.activation(
        rrms1, zsq, AF.Ln, scale=1.0 / DM, bias=1e-6), deps=rs_toks,
        flush=True)
    t5 = p.op("scalar", lambda eng: eng.activation(
        rrms1, rrms1, AF.Exp, scale=-0.5), deps=[t4], flush=True)
    prev_zrd = None
    wtr = []
    for tc in range(NT):
        xld = p.dma("gpsimd", lambda eng, tc=tc: eng.dma_start(
            out=xb[:, :], in_=xio["src"][tc]),
            deps=[ctx.last.get(pref + "_xb_rd")], ch=pref + "_x1")
        zt = ln_apply(ctx, xb[:, :], zb[:, :], stash, tc,
                      deps=[xld, ctx.last.get(pref + "_zb_rd"), t5])
        ctx.last[pref + "_xb_rd"] = zt
        tw1 = p.op("vector", lambda eng, tc=tc: eng.tensor_scalar_mul(
            zb[:, :], zb[:, :], rrms1[:, tc:tc + 1]), deps=[zt])
        tw2 = p.op("vector", lambda eng: eng.tensor_mul(
            ztile[:, :], zb[:, :], n1bc[:, :]), deps=[tw1, tn1])
        ctx.last[pref + "_zb_rd"] = tw2
        for kc in range(8):
            wtr.append(tr128(
                ctx, wfm[:, kc * S + tc * 128: kc * S + (tc + 1) * 128],
                ztile[:, kc * 128:(kc + 1) * 128], consts, deps=[tw2]))
        prev_zrd = tensor_now()

    att_done, qk_free, qkwo = attn_pairs(
        ctx, hb, consts, dict(
            pref=pref, wq="L2f_wq", wk="L2f_wk", wv="L2f_wv",
            scale=1.0 / 8.0, causal=True),
        wfm[:, :], att[:, :], wtr)
    all_done = [qk_free, ("cv", p.cnt["cv"]), ("cs", p.cnt["cs"])]
    wo_sb = qkwo[:, :]
    wo_toks = []
    for kc in range(8):
        wo_toks.append(p.dma("gpsimd", lambda eng, kc=kc: eng.dma_start(
            out=wo_sb[:, kc * DM:(kc + 1) * DM],
            in_=hb["L2f_wo"][kc * 128:(kc + 1) * 128, :]),
            deps=all_done, ch=pref + "_wo"))

    y1sb = ctx.sb(pref + "_y1sb", [128, DM], F32)
    y2_toks = []
    prev_y1st = None
    for tc in range(NT):
        xld = p.dma("gpsimd", lambda eng, tc=tc: eng.dma_start(
            out=xb[:, :], in_=xio["src"][tc]),
            deps=[ctx.last.get(pref + "_xb_rd")], ch=pref + "_x1")
        zt = ln_apply(ctx, xb[:, :], zb[:, :], stash, tc,
                      deps=[xld, ctx.last.get(pref + "_zb_rd")])
        ctx.last[pref + "_xb_rd"] = zt
        ptoks = []
        for osl in range(2):
            bank, bdeps = pool.acquire(F32)
            pairs = [(att[:, kc * S + tc * 128: kc * S + (tc + 1) * 128],
                      wo_sb[:, kc * DM + osl * 512: kc * DM + osl * 512 + 512])
                     for kc in range(8)]
            tmm = mm_accum(ctx, bank["t"][:, :], pairs,
                           deps=[att_done] + wo_toks, bank_deps=bdeps)
            tv = p.op("vector", lambda eng, bank=bank, osl=osl:
                      eng.tensor_add(y1sb[:, osl * 512:(osl + 1) * 512],
                                     bank["t"][:, :],
                                     zb[:, osl * 512:(osl + 1) * 512]),
                      deps=[tmm, zt, prev_y1st])
            pool.release(bank, tv)
            ptoks.append(tv)
        ctx.last[pref + "_zb_rd"] = ptoks[-1]
        tsq = p.op("scalar", lambda eng, tc=tc: eng.activation(
            sqd[:, :], y1sb[:, :], AF.Square, accum_out=y2sq[:, tc:tc + 1]),
            deps=ptoks)
        y2_toks.append(tsq)
        prev_y1st = p.dma("sync", lambda eng, tc=tc: eng.dma_start(
            out=y1v[tc], in_=y1sb[:, :]), deps=ptoks + [tsq],
            ch=pref + "_y1s")
    t6 = p.op("scalar", lambda eng: eng.activation(
        rrms2, y2sq, AF.Ln, scale=1.0 / DM, bias=1e-6), deps=y2_toks,
        flush=True)
    t7 = p.op("scalar", lambda eng: eng.activation(
        rrms2, rrms2, AF.Exp, scale=-0.5), deps=[t6], flush=True)

    x2fm = att
    x2_ready = []
    ch_y1l = pref + "_y1l"
    att_free = tensor_now()
    for tc in range(NT):
        yld = p.dma("gpsimd", lambda eng, tc=tc: eng.dma_start(
            out=y1sb[:, :], in_=y1v[tc]),
            deps=[prev_y1st, ctx.last.get(pref + "_y1sb_rd")], ch=ch_y1l)
        tw1 = p.op("vector", lambda eng, tc=tc: eng.tensor_scalar_mul(
            zb[:, :], y1sb[:, :], rrms2[:, tc:tc + 1]),
            deps=[yld, t7, ctx.last.get(pref + "_zb_rd")])
        ctx.last[pref + "_y1sb_rd"] = tw1
        tw2 = p.op("vector", lambda eng: eng.tensor_mul(
            ztile[:, :], zb[:, :], n2bc[:, :]), deps=[tw1, tn2])
        ctx.last[pref + "_zb_rd"] = tw2
        trs = []
        for kc in range(8):
            trs.append(tr128(
                ctx, x2fm[:, kc * S + tc * 128: kc * S + (tc + 1) * 128],
                ztile[:, kc * 128:(kc + 1) * 128], consts,
                deps=[tw2, att_free]))
        x2_ready += trs
        bank, bdeps = pool.acquire(F32)
        pairs = [(x2fm[:, kc * S + tc * 128: kc * S + (tc + 1) * 128],
                  gwt[:, kc * 8:(kc + 1) * 8]) for kc in range(8)]
        tmm = mm_accum(ctx, bank["t"][:, 0:8], pairs,
                       deps=trs + [tgw], bank_deps=bdeps)
        tg0 = p.op("scalar", lambda eng, bank=bank: eng.copy(
            gsb[:, 0:8], bank["t"][:, 0:8]),
            deps=[tmm, ctx.last.get(pref + "_gsb_rd")], flush=True)
        pool.release(bank, tg0)
        tg1 = p.op("vector", lambda eng: eng.reduce_max(
            gsb[:, 8:9], gsb[:, 0:8], axis=AX.X, negate=True),
            deps=[tg0], flush=True)
        tg2 = p.op("scalar", lambda eng: eng.activation(
            gsb[:, 0:8], gsb[:, 0:8], AF.Exp, bias=gsb[:, 8:9],
            accum_out=gsb[:, 9:10]), deps=[tg1], flush=True)
        tg3 = p.op("vector", lambda eng: eng.reduce_max(
            gsb[:, 10:11], gsb[:, 0:8], axis=AX.X), deps=[tg2], flush=True)
        tg4 = p.op("vector", lambda eng: eng.tensor_scalar(
            gsb[:, 12:20], gsb[:, 0:8], gsb[:, 10:11], None, ALU.is_lt),
            deps=[tg3], flush=True)
        tg5 = p.op("vector", lambda eng: eng.tensor_mul(
            gsb[:, 12:20], gsb[:, 0:8], gsb[:, 12:20]), deps=[tg4],
            flush=True)
        tg6 = p.op("vector", lambda eng: eng.reduce_max(
            gsb[:, 20:21], gsb[:, 12:20], axis=AX.X), deps=[tg5], flush=True)
        tg7 = p.op("vector", lambda eng: eng.tensor_add(
            gsb[:, 21:22], gsb[:, 10:11], gsb[:, 20:21]), deps=[tg6],
            flush=True)
        tg8 = p.op("scalar", lambda eng: eng.copy(
            gsb[:, 22:23], gsb[:, 9:10]), deps=[tg2, tg7], flush=True)
        tg9 = p.op("vector", lambda eng: eng.reciprocal(
            gsb[:, 23:24], gsb[:, 22:23]), deps=[tg8], flush=True)
        tga = p.op("vector", lambda eng: eng.tensor_mul(
            gsb[:, 8:9], gsb[:, 21:22], gsb[:, 23:24]), deps=[tg9],
            flush=True)
        tgb = p.op("vector", lambda eng: eng.tensor_scalar_add(
            gsb[:, 10:11], gsb[:, 8:9], 1e-9), deps=[tga], flush=True)
        tgc = p.op("vector", lambda eng: eng.reciprocal(
            gsb[:, 11:12], gsb[:, 10:11]), deps=[tgb], flush=True)
        tgd = p.op("vector", lambda eng, tc=tc: eng.tensor_mul(
            gstash[:, tc:tc + 1], gsb[:, 8:9], gsb[:, 11:12]), deps=[tgc],
            flush=True)
        ctx.last[pref + "_gsb_rd"] = tgd

    hfm = wfm[:, 0:32 * TW]
    kbuf = [qkwo[:, 0:8 * 512],
            ctx.sb(pref + "_vfm", [128, S], BF16)]
    kbtok = [None, None]
    up_v = hb["L2f_up"]
    dn_v = hb["L2f_dn"]
    osb = y1sb
    xbo = xb
    wfm_free = tensor_now()

    def kload2(b, view, deps=()):
        kbtok[b] = p.dma(
            "gpsimd",
            lambda eng, b=b, view=view: eng.dma_start(
                out=kbuf[b][:, 0:8 * 512].rearrange("p (k c) -> p k c", c=512),
                in_=view.rearrange("k p c -> p k c")),
            deps=[kbtok[b], ctx.last.get(pref + f"_kb{b}_rd")] + list(deps),
            ch=pref + f"_kb{b}")
        return kbtok[b]

    prev_ost = [None, None]
    for w in range(NW):
        t0 = w * 2
        h_toks = []
        for ob in range(8):
            b = ob % 2
            wt = kload2(b, up_v[ob], deps=[wfm_free])
            for j in range(4):
                bank, bdeps = pool.acquire(F32)
                pairs = [(kbuf[b][:, kc * 512 + j * 128:
                                  kc * 512 + (j + 1) * 128],
                          x2fm[:, kc * S + t0 * 128:
                               kc * S + (t0 + 2) * 128]) for kc in range(8)]
                tmm = mm_accum(ctx, bank["t"][:, 0:TW], pairs,
                               deps=x2_ready + [wt], bank_deps=bdeps)
                cglob = ob * 4 + j
                tev = p.op("scalar", lambda eng, bank=bank, cglob=cglob:
                           eng.activation(hfm[:, cglob * TW:(cglob + 1) * TW],
                                          bank["t"][:, 0:TW], AF.Silu),
                           deps=[tmm])
                pool.release(bank, tev)
                h_toks.append(tev)
            ctx.last[pref + f"_kb{b}_rd"] = tensor_now()
        held = {}
        for ob3 in range(2):
            for ti in range(2):
                held[(ti, ob3)] = pool.acquire(F32)
        for ob3 in range(2):
            for sub in range(4):
                b = sub % 2
                wt = kload2(b, dn_v[ob3, sub * 8:(sub + 1) * 8])
                for ti in range(2):
                    bank, bdeps = held[(ti, ob3)]
                    pairs = [(hfm[:, (sub * 8 + kc) * TW + ti * 128:
                                  (sub * 8 + kc) * TW + (ti + 1) * 128],
                              kbuf[b][:, kc * 512:(kc + 1) * 512])
                             for kc in range(8)]
                    mm_accum(ctx, bank["t"][:, :], pairs,
                             deps=h_toks + [wt],
                             bank_deps=bdeps if sub == 0 else [],
                             start=(sub == 0), stop=(sub == 3))
                ctx.last[pref + f"_kb{b}_rd"] = tensor_now()
        dn_done = tensor_now()
        for ti in range(2):
            tc = t0 + ti
            xld = p.dma("gpsimd", lambda eng, tc=tc: eng.dma_start(
                out=xbo[:, :], in_=xio["src"][tc]),
                deps=[ctx.last.get(pref + "_xb_rd")], ch=pref + "_x1")
            yld = p.dma("sync", lambda eng, tc=tc: eng.dma_start(
                out=zb[:, :], in_=y1v[tc]),
                deps=[prev_y1st, ctx.last.get(pref + "_zb_rd")],
                ch=pref + "_y1l2")
            ptoks = []
            for ob3 in range(2):
                bank, _ = held[(ti, ob3)]
                osl_ap = osb[:, ob3 * 512:(ob3 + 1) * 512]
                tv1 = p.op("vector", lambda eng, bank=bank, o=osl_ap, tc=tc,
                           ob3=ob3:
                           eng.scalar_tensor_tensor(
                               o, bank["t"][:, :], gstash[:, tc:tc + 1],
                               zb[:, ob3 * 512:(ob3 + 1) * 512],
                               ALU.mult, ALU.add),
                           deps=[dn_done, yld, tgd,
                                 ctx.last.get(pref + "_osb_rd")])
                tv2 = p.op("vector", lambda eng, o=osl_ap, ob3=ob3:
                           eng.tensor_add(o, o,
                                          xbo[:, ob3 * 512:(ob3 + 1) * 512]),
                           deps=[tv1, xld])
                pool.release(bank, tv2)
                ptoks.append(tv2)
            ctx.last[pref + "_xb_rd"] = ptoks[-1]
            ctx.last[pref + "_zb_rd"] = ptoks[-1]
            tost = p.dma("sync", lambda eng, tc=tc: eng.dma_start(
                out=xio["dst"][tc], in_=osb[:, :]), deps=ptoks,
                ch=pref + "_xso")
            ctx.last[pref + "_osb_rd"] = tost
    return




BLOCK_EMITTERS["l2_ffn"] = emit_l2_ffn


def prep_l2_ffn(inputs):
    WT = np.ascontiguousarray(np.asarray(inputs["L2_ffn_qkv_w"],
                                         np.float32).T)
    upT = np.ascontiguousarray(np.asarray(inputs["L2_ffn_up_w"],
                                          np.float32).T)
    dnT = np.ascontiguousarray(np.asarray(inputs["L2_ffn_down_w"],
                                          np.float32).T)
    up_t = np.ascontiguousarray(
        upT.reshape(8, 128, 8, 512).transpose(2, 0, 1, 3))
    dn_t = np.ascontiguousarray(
        dnT.reshape(32, 128, 2, 512).transpose(2, 0, 1, 3))
    gwT = np.ascontiguousarray(np.asarray(inputs["L2_ffn_gate_w"],
                                          np.float32).T).reshape(8, 128, 8)
    specs = {
        "L2f_wq": ((8, 1024, 128), F32),
        "L2f_wk": ((8, 1024, 128), F32),
        "L2f_wv": ((8, 1024, 128), F32),
        "L2f_wo": ((1024, 1024), F32),
        "L2f_n1": ((128, 1024), F32),
        "L2f_n2": ((128, 1024), F32),
        "L2f_gw": ((8, 128, 8), F32),
        "L2f_up": ((8, 8, 128, 512), F32),
        "L2f_dn": ((2, 32, 128, 512), F32),
    }
    specs.update(COMMON_SPECS)
    arrs = {
        "L2f_wq": _pair_tile(np.ascontiguousarray(WT[:, 0:1024])),
        "L2f_wk": _pair_tile(np.ascontiguousarray(WT[:, 1024:2048])),
        "L2f_wv": _pair_tile(np.ascontiguousarray(WT[:, 2048:3072])),
        "L2f_wo": np.ascontiguousarray(
            np.asarray(inputs["L2_ffn_o_w"], np.float32).T),
        "L2f_n1": np.ascontiguousarray(np.broadcast_to(
            np.asarray(inputs["L2_ffn_n1_w"], np.float32), (128, 1024))),
        "L2f_n2": np.ascontiguousarray(np.broadcast_to(
            np.asarray(inputs["L2_ffn_n2_w"], np.float32), (128, 1024))),
        "L2f_gw": gwT,
        "L2f_up": up_t,
        "L2f_dn": dn_t,
    }
    arrs.update(common_arrays())
    return specs, arrs





def emit_l1_ffn(ctx, xio, hb, consts):
    nc, p = ctx.nc, ctx.p
    pref = "l1f"
    pool = ctx.pool

    def tensor_now():
        return ("ct", p.cnt["ct"])

    zfm = ctx.sb(pref + "_zfm", [128, 8 * S], BF16)
    stash = ctx.sb(pref + "_stash", [128, 64], F32)
    nmb = ctx.sb(pref + "_nmb", [128, 32], F32)
    att = ctx.sb(pref + "_att", [128, 2 * S], BF16)
    qsel = ctx.sb(pref + "_qsel", [128, 2 * S], BF16)
    ksel = ctx.sb(pref + "_ksel", [128, 2 * S], BF16)
    vsel = ctx.sb(pref + "_vsel", [128, 2 * S], BF16)
    vaug = [ctx.sb(pref + f"_va{h}", [128, 32 * 65], BF16) for h in range(2)]
    wsel = [ctx.sb(pref + f"_ws{j}", [128, 8 * 256], BF16) for j in range(3)]
    bsel = ctx.sb(pref + "_bs", [128, 8], F32)
    gvec = ctx.sb(pref + "_gv", [128, 2], F32)
    rsb = ctx.sb(pref + "_rsb", [128, 96], F32)
    rib = ctx.sb(pref + "_rib", [16, 80], F32)
    riw = ctx.sb(pref + "_riw", [128, 8 * 16], F32)
    selint = ctx.sb(pref + "_si", [1, 64], I32)
    ptile = [ctx.sb(pref + f"_pt{j}", [128, 512], BF16) for j in range(2)]
    nsb = ctx.sb(pref + "_nsb", [128, 512], F32)
    rrow = ctx.sb(pref + "_rr", [1, 1024], F32)
    wo_sb = ctx.sb(pref + "_wo", [128, 2 * DM], BF16)
    bo = ctx.sb(pref + "_bo", [128, DM], F32)
    onescol = ctx.sb(pref + "_onec", [128, 2], BF16)
    zsum_d = nc.dram_tensor("l1f_zsum", [1024], F32, kind="Internal").ap()
    sel_d = nc.dram_tensor("l1f_sel", [64], I32, kind="Internal").ap()
    ones = consts["ones_f32"]

    tbo = p.dma("sync", lambda eng: eng.dma_start(
        out=bo[:, :], in_=hb["L1f_bo"][:, :]), ch=pref + "_bo")
    twr = p.dma("sync", lambda eng: eng.dma_start(
        out=riw[:, :].rearrange("p (k c) -> p k c", c=16),
        in_=hb["L1f_wr"][:, :, :].rearrange("k p c -> p k c")),
        ch=pref + "_wr")
    tone = p.op("vector", lambda eng: eng.memset(onescol[:, :], 1.0),
                deps=[], flush=True)

    tstats = emit_ln_stats(ctx, xio, pref, stash[:, :], nmb)
    xbz = ctx.sb(pref + "_sxb0", [128, DM], F32)
    ztile = ctx.sb(pref + "_zt", [128, DM], BF16)
    prev_zrd = None
    ztr = []
    zs_banks = [pool.acquire(F32) for _ in range(2)]
    for tc in range(NT):
        xld = p.dma("gpsimd", lambda eng, tc=tc: eng.dma_start(
            out=xbz[:, :], in_=xio["src"][tc]),
            deps=[prev_zrd, ctx.last.get(pref + "_zap"), tstats],
            ch=pref + "_xz")
        zt = ln_apply(ctx, xbz[:, :], ztile[:, :], stash, tc,
                      deps=[xld, tstats, prev_zrd])
        ctx.last[pref + "_zap"] = zt
        for half in range(2):
            bank, bdeps = zs_banks[half]
            mm_accum(ctx, bank["t"][0:1, :],
                     [(onescol[:, 0:1],
                       ztile[:, half * 512:(half + 1) * 512])],
                     deps=[zt, tone],
                     bank_deps=bdeps if tc == 0 else [],
                     start=(tc == 0), stop=(tc == NT - 1))
        for kc in range(8):
            ztr.append(tr128(
                ctx, zfm[:, kc * S + tc * 128: kc * S + (tc + 1) * 128],
                ztile[:, kc * 128:(kc + 1) * 128], consts, deps=[zt]))
        prev_zrd = tensor_now()
    zs_done = tensor_now()
    tze = []
    for half in range(2):
        bank, _ = zs_banks[half]
        t = p.op("scalar", lambda eng, bank=bank, half=half: eng.copy(
            rrow[0:1, half * 512:(half + 1) * 512], bank["t"][0:1, :]),
            deps=[zs_done], flush=True)
        pool.release(bank, t)
        tze.append(t)
    tzs = p.dma("sync", lambda eng: eng.dma_start(
        out=zsum_d[:], in_=rrow[0:1, 0:1024]), deps=tze, ch=pref + "_zs")
    tzl = p.dma("gpsimd", lambda eng: eng.dma_start(
        out=rsb[:, 0:8],
        in_=zsum_d.rearrange("(k p) -> p k", p=128),
        allow_slow_non_contiguous=True), deps=[tzs],
        ch=pref + "_zl")
    zc16 = ctx.sb(pref + "_zc16", [128, 8], F32)
    tzc = p.op("vector", lambda eng: eng.tensor_scalar_mul(
        zc16[:, :], rsb[:, 0:8], 1.0 / S), deps=[tzl], flush=True)
    bank, bdeps = pool.acquire(F32)
    tlg = mm_accum(ctx, bank["t"][0:1, 0:16],
                   [(zc16[:, kc:kc + 1], riw[:, kc * 16:(kc + 1) * 16])
                    for kc in range(8)],
                   deps=[tzc, twr], bank_deps=bdeps)
    tbr = p.dma("sync", lambda eng: eng.dma_start(
        out=rsb[0:1, 16:32], in_=hb["L1f_br"][:, :]), ch=pref + "_br")
    t1 = p.op("vector", lambda eng, bank=bank: eng.tensor_add(
        rsb[0:1, 32:48], bank["t"][0:1, 0:16], rsb[0:1, 16:32]),
        deps=[tlg, tbr], flush=True)
    pool.release(bank, t1)
    t2 = p.op("vector", lambda eng: eng.reduce_max(
        rsb[0:1, 48:49], rsb[0:1, 32:48], axis=AX.X, negate=True),
        deps=[t1], flush=True)
    t3 = p.op("scalar", lambda eng: eng.activation(
        rsb[0:1, 32:48], rsb[0:1, 32:48], AF.Exp, bias=rsb[0:1, 48:49],
        accum_out=rsb[0:1, 49:50]), deps=[t2], flush=True)
    t4 = p.op("scalar", lambda eng: eng.copy(
        rsb[0:1, 50:51], rsb[0:1, 49:50]), deps=[t3], flush=True)
    t5 = p.op("vector", lambda eng: eng.reciprocal(
        rsb[0:1, 51:52], rsb[0:1, 50:51]), deps=[t4], flush=True)
    t6 = p.op("vector", lambda eng: eng.tensor_scalar_mul(
        rsb[0:1, 32:48], rsb[0:1, 32:48], rsb[0:1, 51:52]),
        deps=[t5], flush=True)
    bkb, bkdeps = pool.acquire(F32)
    tb1 = mm_accum(ctx, bkb["t"][0:16, 0:16],
                   [(ones[0:1, 0:16], rsb[0:1, 32:48])],
                   deps=[t6], bank_deps=bkdeps)
    tb2 = p.op("scalar", lambda eng, bkb=bkb: eng.copy(
        rib[0:16, 0:16], bkb["t"][0:16, 0:16]), deps=[tb1], flush=True)
    pool.release(bkb, tb2)
    trp = tr128(ctx, rib[0:16, 16:17], rsb[0:1, 32:48], consts, deps=[t6],
                dtype=F32, evac_engine="vector")
    tr1 = p.op("vector", lambda eng: eng.tensor_scalar(
        rib[0:16, 17:33], rib[0:16, 0:16], rib[0:16, 16:17], None,
        ALU.is_gt), deps=[tb2, trp], flush=True)
    tr2 = p.op("vector", lambda eng: eng.tensor_scalar(
        rib[0:16, 33:49], rib[0:16, 0:16], rib[0:16, 16:17], None,
        ALU.is_equal), deps=[tr1], flush=True)
    tlt = p.dma("sync", lambda eng: eng.dma_start(
        out=rib[0:16, 49:65], in_=hb["L1f_lt"][:, :]), ch=pref + "_lt")
    tr3 = p.op("vector", lambda eng: eng.tensor_mul(
        rib[0:16, 33:49], rib[0:16, 33:49], rib[0:16, 49:65]),
        deps=[tr2, tlt], flush=True)
    tr4 = p.op("vector", lambda eng: eng.tensor_add(
        rib[0:16, 17:33], rib[0:16, 17:33], rib[0:16, 33:49]),
        deps=[tr3], flush=True)
    tr5 = p.op("vector", lambda eng: eng.reduce_sum(
        rib[0:16, 65:66], rib[0:16, 17:33], axis=AX.X), deps=[tr4],
        flush=True)
    tki = p.dma("sync", lambda eng: eng.dma_start(
        out=rib[0:16, 66:70], in_=hb["L1f_ki"][:, :]), ch=pref + "_ki")
    ts1 = p.op("scalar", lambda eng: eng.activation(
        rib[0:16, 70:71], rib[0:16, 65:66], AF.Copy, scale=1.0),
        deps=[tr5], flush=True)
    ts2 = p.op("vector", lambda eng: eng.tensor_scalar(
        rib[0:16, 66:70], rib[0:16, 66:70], rib[0:16, 70:71], None,
        ALU.is_equal), deps=[ts1, tki], flush=True)
    tm1 = p.op("vector", lambda eng: eng.tensor_scalar(
        rib[0:16, 71:72], rib[0:16, 70:71], 4.0, None, ALU.is_lt),
        deps=[ts2], flush=True)
    trm = tr128(ctx, rsb[0:1, 52:68], rib[0:16, 71:72], consts,
                deps=[tm1], dtype=F32, evac_engine="vector")
    tg1 = p.op("vector", lambda eng: eng.tensor_mul(
        rsb[0:1, 52:68], rsb[0:1, 32:48], rsb[0:1, 52:68]),
        deps=[trm], flush=True)
    tg2 = p.op("vector", lambda eng: eng.reduce_max(
        rsb[0:1, 48:49], rsb[0:1, 52:68], axis=AX.X, negate=True),
        deps=[tg1], flush=True)
    tg3 = p.op("scalar", lambda eng: eng.activation(
        rsb[0:1, 52:68], rsb[0:1, 52:68], AF.Exp, bias=rsb[0:1, 48:49],
        accum_out=rsb[0:1, 49:50]), deps=[tg2], flush=True)
    tg4 = p.op("scalar", lambda eng: eng.copy(
        rsb[0:1, 50:51], rsb[0:1, 49:50]), deps=[tg3], flush=True)
    tg5 = p.op("vector", lambda eng: eng.reciprocal(
        rsb[0:1, 51:52], rsb[0:1, 50:51]), deps=[tg4], flush=True)
    tg6 = p.op("vector", lambda eng: eng.tensor_scalar_mul(
        rsb[0:1, 52:68], rsb[0:1, 52:68], rsb[0:1, 51:52]),
        deps=[tg5], flush=True)
    tgt = tr128(ctx, rib[0:16, 72:73], rsb[0:1, 52:68], consts,
                deps=[tg6], dtype=F32, evac_engine="vector")
    bkg, bkgd = pool.acquire(F32)
    tgm = mm_accum(ctx, bkg["t"][0:4, 0:1],
                   [(rib[0:16, 66:70], rib[0:16, 72:73])],
                   deps=[ts2, tgt], bank_deps=bkgd)
    tge = p.op("scalar", lambda eng, bkg=bkg: eng.copy(
        rib[0:4, 73:74], bkg["t"][0:4, 0:1]), deps=[tgm], flush=True)
    pool.release(bkg, tge)
    tgr = tr128(ctx, rsb[0:1, 68:72], rib[0:4, 73:74], consts,
                deps=[tge], dtype=F32, evac_engine="vector")
    bkh, bkhd = pool.acquire(F32)
    tgb = mm_accum(ctx, bkh["t"][0:128, 0:4],
                   [(ones[0:1, 0:128], rsb[0:1, 68:72])],
                   deps=[tgr], bank_deps=bkhd)
    gts = []
    for c in range(2):
        for hh in range(2):
            t = p.op("vector", lambda eng, bkh=bkh, c=c, hh=hh:
                     eng.tensor_copy(
                         gvec[hh * 64:(hh + 1) * 64, c:c + 1],
                         bkh["t"][hh * 64:(hh + 1) * 64, c * 2 + hh:
                                  c * 2 + hh + 1]),
                     deps=[tgb], flush=True)
            gts.append(t)
    pool.release(bkh, gts[-1])
    sint16 = ctx.sb(pref + "_si16", [16, 4], I32)
    tsic = p.op("vector", lambda eng: eng.tensor_copy(
        sint16[:, :], rib[0:16, 66:70]), deps=[ts2], flush=True)
    tsd = p.dma("sync", lambda eng: eng.dma_start(
        out=sel_d[:], in_=sint16[:, :]), deps=[tsic], ch=pref + "_sd")
    tsl = p.dma("gpsimd", lambda eng: eng.dma_start(
        out=selint[:, :], in_=sel_d.rearrange("(a c) -> a c", a=1)),
        deps=[tsd], ch=pref + "_sl")

    wg_toks = []
    for j, wkey in enumerate(("L1f_wq", "L1f_wk", "L1f_wv")):
        for k in range(4):
            arms = []
            for h in range(16):
                flag = selint[0:1, h * 4 + k:h * 4 + k + 1]
                dst = wsel[j][:, :].rearrange(
                    "p (kc s c) -> p kc s c", s=4, c=64)[:, :, k, :]
                srcv = hb[wkey][h].rearrange("(kc p) c -> p kc c", p=128)
                arms.append((flag, lambda eng, dst=dst, srcv=srcv:
                             eng.dma_start(out=dst, in_=srcv)))
            wg_toks.append(p.route("gpsimd", arms, deps=[tsl],
                                   ch=pref + f"_wg{j}_{k}"))
    bg_toks = []
    for j, bkey in enumerate(("L1f_bq", "L1f_bk", "L1f_bv")):
        for k in range(4):
            arms = []
            for h in range(16):
                flag = selint[0:1, h * 4 + k:h * 4 + k + 1]
                dst = bsel[(k % 2) * 64:(k % 2) * 64 + 64,
                           j + (k // 2) * 4:j + (k // 2) * 4 + 1]
                srcv = hb[bkey][h].rearrange("(a p) -> p a", a=1)
                arms.append((flag, lambda eng, dst=dst, srcv=srcv:
                             eng.dma_start(out=dst, in_=srcv)))
            bg_toks.append(p.route("gpsimd", arms, deps=[tsl],
                                   ch=pref + f"_bg{j}_{k}"))
    tbs0 = p.op("vector", lambda eng: eng.tensor_scalar(
        bsel[:, 0:4], bsel[:, 0:4], gvec[:, 0:1], None, ALU.mult),
        deps=bg_toks + gts, flush=True)
    tbs = p.op("vector", lambda eng: eng.tensor_scalar(
        bsel[:, 4:8], bsel[:, 4:8], gvec[:, 1:2], None, ALU.mult),
        deps=[tbs0], flush=True)

    proj_toks = []
    for j, dstbuf in enumerate((qsel, ksel, vsel)):
        for ch2 in range(2):
            for tsl in range(8):
                bank, bdeps = pool.acquire(F32)
                pairs = [(wsel[j][:, :].rearrange(
                            "p (kc cc) -> p kc cc", cc=256)[
                            :, kc, ch2 * 128:(ch2 + 1) * 128],
                          zfm[:, kc * S + tsl * 512: kc * S + tsl * 512 + 512])
                         for kc in range(8)]
                tmm = mm_accum(ctx, bank["t"][:, :], pairs,
                               deps=ztr + [wg_toks[j * 4], wg_toks[j * 4 + 1],
                                           wg_toks[j * 4 + 2],
                                           wg_toks[j * 4 + 3]],
                               bank_deps=bdeps)
                tev = p.op("scalar",
                           lambda eng, bank=bank, dstbuf=dstbuf, tsl=tsl,
                           ch2=ch2, j=j:
                           eng.activation(
                               dstbuf[:, ch2 * S + tsl * 512:
                                      ch2 * S + tsl * 512 + 512],
                               bank["t"][:, :], AF.Identity,
                               scale=gvec[:, ch2:ch2 + 1],
                               bias=bsel[:, j + ch2 * 4:j + ch2 * 4 + 1]),
                           deps=[tmm, tbs])
                pool.release(bank, tev)
                proj_toks.append(tev)
    vone = []
    for hh in range(2):
        ap = vaug[hh][:, :].rearrange("p (n c) -> p n c", c=65)[:, :, 64:65]
        vone.append(p.op("vector", lambda eng, ap=ap: eng.memset(ap, 1.0),
                         deps=[], flush=True))
    for slot in range(4):
        ch2 = slot // 2
        hh = slot % 2
        hb64 = hh * 64
        vtr = []
        for tc in range(NT):
            vtr.append(tr128(
                ctx, vaug[hh][:, tc * 65: tc * 65 + 64],
                vsel[hb64:hb64 + 64,
                     ch2 * S + tc * 128: ch2 * S + (tc + 1) * 128],
                consts, deps=[proj_toks[32 + ch2 * 8 + tsl_of(tc)], vone[hh],
                              ctx.last.get(pref + f"_va{hh}_rd")],
                src_base=hb64))
        for tsl in range(8):
            pvb, pvdeps = pool.acquire(F32)
            for jc in range(32):
                scb, scdeps = pool.acquire(F32)
                tmm = mm_accum(
                    ctx, scb["t"][:, :],
                    [(ksel[hb64:hb64 + 64,
                           ch2 * S + jc * 128: ch2 * S + (jc + 1) * 128],
                      qsel[hb64:hb64 + 64,
                           ch2 * S + tsl * 512: ch2 * S + tsl * 512 + 512])],
                    deps=proj_toks, bank_deps=scdeps)
                b = jc % 2
                texp = p.op("scalar",
                            lambda eng, scb=scb, b=b: eng.activation(
                                ptile[b][:, :], scb["t"][:, :], AF.Exp,
                                scale=1.0 / 8.0),
                            deps=[tmm, ctx.last.get(pref + f"_pt{b}_rd")])
                pool.release(scb, texp)
                tpv = mm_accum(
                    ctx, pvb["t"][0:65, :],
                    [(vaug[hh][:, jc * 65:(jc + 1) * 65],
                      ptile[b][:, :])],
                    deps=[texp] + vtr,
                    bank_deps=pvdeps if jc == 0 else [],
                    start=(jc == 0), stop=(jc == 31))
                ctx.last[pref + f"_pt{b}_rd"] = tpv
            tev = p.op("scalar", lambda eng, pvb=pvb: eng.copy(
                nsb[0:65, :], pvb["t"][0:65, :]),
                deps=[tpv, ctx.last.get(pref + "_nsb_rd")])
            pool.release(pvb, tev)
            trc = p.op("vector", lambda eng: eng.reciprocal(
                rrow[0:1, 0:512], nsb[64:65, :]), deps=[tev], flush=True)
            bcb, bcdeps = pool.acquire(F32)
            tbc = mm_accum(ctx, bcb["t"][0:64, :],
                           [(ones[0:1, 0:64], rrow[0:1, 0:512])],
                           deps=[trc], bank_deps=bcdeps)
            dst = att[hb64:hb64 + 64, ch2 * S + tsl * 512:
                      ch2 * S + tsl * 512 + 512]
            tmu = p.op("vector", lambda eng, dst=dst, bcb=bcb:
                       eng.tensor_mul(dst, nsb[0:64, :],
                                      bcb["t"][0:64, :]),
                       deps=[tbc])
            pool.release(bcb, tmu)
            ctx.last[pref + "_nsb_rd"] = tmu
        ctx.last[pref + f"_va{hh}_rd"] = tensor_now()
    att_done = ("cv", p.cnt["cv"])

    two = []
    for kc in range(2):
        two.append(p.dma("gpsimd", lambda eng, kc=kc: eng.dma_start(
            out=wo_sb[:, kc * DM:(kc + 1) * DM],
            in_=hb["L1f_wo"][kc * 128:(kc + 1) * 128, :]), ch=pref + "_wo"))
    xbo = xbz
    osb = ctx.sb(pref + "_osb", [128, DM], F32)
    prev_ost = None
    for tc in range(NT):
        xld = p.dma("gpsimd", lambda eng, tc=tc: eng.dma_start(
            out=xbo[:, :], in_=xio["src"][tc]),
            deps=[ctx.last.get(pref + "_xbo_rd")], ch=pref + "_xo")
        ptoks = []
        for osl in range(2):
            bank, bdeps = pool.acquire(F32)
            pairs = [(att[:, kc * S + tc * 128: kc * S + (tc + 1) * 128],
                      wo_sb[:, kc * DM + osl * 512: kc * DM + osl * 512 + 512])
                     for kc in range(2)]
            tmm = mm_accum(ctx, bank["t"][:, :], pairs,
                           deps=[att_done] + two, bank_deps=bdeps)
            osl_ap = osb[:, osl * 512:(osl + 1) * 512]
            xsl_ap = xbo[:, osl * 512:(osl + 1) * 512]
            tv = p.op("vector", lambda eng, bank=bank, o=osl_ap, xs=xsl_ap:
                      eng.tensor_add(o, bank["t"][:, :], xs),
                      deps=[tmm, xld, prev_ost])
            pool.release(bank, tv)
            tb2 = p.op("vector", lambda eng, o=osl_ap, osl=osl:
                       eng.tensor_add(o, o, bo[:, osl * 512:(osl + 1) * 512]),
                       deps=[tv, tbo])
            ptoks.append(tb2)
        ctx.last[pref + "_xbo_rd"] = ptoks[-1]
        prev_ost = p.dma("sync", lambda eng, tc=tc: eng.dma_start(
            out=xio["dst"][tc], in_=osb[:, :]),
            deps=ptoks, ch=pref + "_xso")
    return


BLOCK_EMITTERS["l1_ffn"] = emit_l1_ffn


def prep_l1_ffn(inputs):
    def headtile(w):
        return np.ascontiguousarray(
            np.asarray(w, np.float32).reshape(1024, 16, 64).transpose(1, 0, 2))
    lt = np.tril(np.ones((16, 16), np.float32), -1)
    specs = {
        "L1f_wq": ((16, 1024, 64), F32),
        "L1f_wk": ((16, 1024, 64), F32),
        "L1f_wv": ((16, 1024, 64), F32),
        "L1f_bq": ((16, 64), F32),
        "L1f_bk": ((16, 64), F32),
        "L1f_bv": ((16, 64), F32),
        "L1f_wr": ((8, 128, 16), F32),
        "L1f_br": ((1, 16), F32),
        "L1f_wo": ((256, 1024), F32),
        "L1f_bo": ((128, 1024), F32),
        "L1f_lt": ((16, 16), F32),
        "L1f_ki": ((16, 4), F32),
    }
    specs.update(COMMON_SPECS)
    arrs = {
        "L1f_wq": headtile(inputs["L1_ffn_Wq"]),
        "L1f_wk": headtile(inputs["L1_ffn_Wk"]),
        "L1f_wv": headtile(inputs["L1_ffn_Wv"]),
        "L1f_bq": np.ascontiguousarray(
            np.asarray(inputs["L1_ffn_bq"], np.float32).reshape(16, 64)),
        "L1f_bk": np.ascontiguousarray(
            np.asarray(inputs["L1_ffn_bk"], np.float32).reshape(16, 64)),
        "L1f_bv": np.ascontiguousarray(
            np.asarray(inputs["L1_ffn_bv"], np.float32).reshape(16, 64)),
        "L1f_wr": np.ascontiguousarray(
            np.asarray(inputs["L1_ffn_Wr"], np.float32).reshape(8, 128, 16)),
        "L1f_br": np.ascontiguousarray(
            np.asarray(inputs["L1_ffn_br"], np.float32).reshape(1, 16)),
        "L1f_wo": np.ascontiguousarray(
            np.asarray(inputs["L1_ffn_Wo"], np.float32)),
        "L1f_bo": np.ascontiguousarray(np.broadcast_to(
            np.asarray(inputs["L1_ffn_bo"], np.float32), (128, 1024))),
        "L1f_lt": lt,
        "L1f_ki": np.ascontiguousarray(np.broadcast_to(
            np.arange(4, dtype=np.float32), (16, 4))),
    }
    arrs.update(common_arrays())
    return specs, arrs





def emit_l0_ffn(ctx, xio, hb, consts):
    nc, p = ctx.nc, ctx.p
    pref = "l0f"
    pool = ctx.pool

    def tensor_now():
        return ("ct", p.cnt["ct"])

    TW = 512
    NW = S // TW
    zfm = ctx.sb(pref + "_zfm", [128, 8 * S], BF16)
    stash = ctx.sb(pref + "_stash", [128, 64], F32)
    nmb = ctx.sb(pref + "_nmb", [128, 32], F32)
    w1s = ctx.sb(pref + "_w1s", [128, 8 * 2048], BF16)
    w2s = ctx.sb(pref + "_w2s", [128, 16 * 1024], BF16)
    wos = ctx.sb(pref + "_wos", [128, 8 * 1024], BF16)
    hfm = ctx.sb(pref + "_hfm", [128, 16 * TW], BF16)
    rfm = ctx.sb(pref + "_rfm", [128, 8 * TW], BF16)
    b1s = ctx.sb(pref + "_b1s", [128, 16], F32)
    beff = ctx.sb(pref + "_beff", [128, DM], F32)
    rsb = ctx.sb(pref + "_rsb", [128, 160], F32)
    oh_i = ctx.sb(pref + "_ohi", [1, 8], I32)
    zl_d = nc.dram_tensor("l0f_zl", [1024], F32, kind="Internal").ap()
    ones = consts["ones_f32"]

    tstats = emit_ln_stats(ctx, xio, pref, stash[:, :], nmb)
    xbz = ctx.sb(pref + "_sxb0", [128, DM], F32)
    zb = ctx.sb(pref + "_sxb1", [128, DM], F32)
    ztile = ctx.sb(pref + "_zt", [128, DM], BF16)
    prev_zrd = None
    ztr = []
    tz31 = None
    for tc in range(NT):
        xld = p.dma("gpsimd", lambda eng, tc=tc: eng.dma_start(
            out=xbz[:, :], in_=xio["src"][tc]),
            deps=[prev_zrd, ctx.last.get(pref + "_zap"), tstats],
            ch=pref + "_xz")
        zt = ln_apply(ctx, xbz[:, :], ztile[:, :], stash, tc,
                      deps=[xld, tstats, prev_zrd])
        ctx.last[pref + "_zap"] = zt
        if tc == NT - 1:
            tzc = p.op("vector", lambda eng: eng.tensor_copy(
                zb[:, :], ztile[:, :]), deps=[zt])
            tz31 = p.dma("sync", lambda eng: eng.dma_start(
                out=zl_d[:], in_=zb[127:128, :]), deps=[tzc],
                ch=pref + "_zl")
            ctx.last[pref + "_zb_rd"] = tz31
        for kc in range(8):
            ztr.append(tr128(
                ctx, zfm[:, kc * S + tc * 128: kc * S + (tc + 1) * 128],
                ztile[:, kc * 128:(kc + 1) * 128], consts, deps=[zt]))
        prev_zrd = tensor_now()

    tzl = p.dma("gpsimd", lambda eng: eng.dma_start(
        out=rsb[:, 0:8], in_=zl_d.rearrange("(k p) -> p k", p=128),
        allow_slow_non_contiguous=True), deps=[tz31], ch=pref + "_zr")
    wp1 = ctx.sb(pref + "_wp1", [128, 8 * 128], F32)
    twp1 = p.dma("sync", lambda eng: eng.dma_start(
        out=wp1[:, :].rearrange("p (k c) -> p k c", c=128),
        in_=hb["L0f_wp1"][:, :, :].rearrange("k p c -> p k c")),
        ch=pref + "_wp1")
    bank, bdeps = pool.acquire(F32)
    tl1 = mm_accum(ctx, bank["t"][0:1, 0:128],
                   [(rsb[:, kc:kc + 1], wp1[:, kc * 128:(kc + 1) * 128])
                    for kc in range(8)],
                   deps=[tzl, twp1], bank_deps=bdeps)
    tb1 = p.dma("sync", lambda eng: eng.dma_start(
        out=rsb[0:1, 8:136], in_=hb["L0f_bp1"][:, :]), ch=pref + "_bp1")
    th1 = p.op("vector", lambda eng, bank=bank: eng.tensor_add(
        rsb[0:1, 8:136], bank["t"][0:1, 0:128], rsb[0:1, 8:136]),
        deps=[tl1, tb1], flush=True)
    pool.release(bank, th1)
    tg1 = p.op("scalar", lambda eng: eng.activation(
        rsb[0:1, 8:136], rsb[0:1, 8:136], AF.Gelu), deps=[th1], flush=True)
    tgt = tr128(ctx, rsb[:, 136:137], rsb[0:1, 8:136], consts, deps=[tg1],
                dtype=F32, evac_engine="vector")
    wp2 = ctx.sb(pref + "_wp2", [128, 8], F32)
    twp2 = p.dma("sync", lambda eng: eng.dma_start(
        out=wp2[:, :], in_=hb["L0f_wp2"][:, :]), ch=pref + "_wp2")
    bank, bdeps = pool.acquire(F32)
    tl2 = mm_accum(ctx, bank["t"][0:1, 0:8],
                   [(rsb[:, 136:137], wp2[:, :])],
                   deps=[tgt, twp2], bank_deps=bdeps)
    tb2 = p.dma("sync", lambda eng: eng.dma_start(
        out=rsb[0:1, 137:145], in_=hb["L0f_bp2"][:, :]), ch=pref + "_bp2")
    tl3 = p.op("vector", lambda eng, bank=bank: eng.tensor_add(
        rsb[0:1, 137:145], bank["t"][0:1, 0:8], rsb[0:1, 137:145]),
        deps=[tl2, tb2], flush=True)
    pool.release(bank, tl3)
    tmx = p.op("vector", lambda eng: eng.reduce_max(
        rsb[0:1, 145:146], rsb[0:1, 137:145], axis=AX.X), deps=[tl3],
        flush=True)
    teq = p.op("vector", lambda eng: eng.tensor_scalar(
        rsb[0:1, 146:154], rsb[0:1, 137:145], rsb[0:1, 145:146], None,
        ALU.is_ge), deps=[tmx], flush=True)
    tsi = p.op("scalar", lambda eng: eng.copy(
        oh_i[0:1, :], rsb[0:1, 146:154]), deps=[teq], flush=True)

    import os
    if os.environ.get("L0F_DBG"):
        dbgr = nc.dram_tensor("dbg_rsb", [128, 160], F32,
                              kind="ExternalOutput").ap()
        p.dma("sync", lambda eng: eng.dma_start(
            out=dbgr[:, :], in_=rsb[:, :]), deps=[tsi], ch="dbg_s")
        dbgo = nc.dram_tensor("dbg_ohi", [1, 8], I32,
                              kind="ExternalOutput").ap()
        p.dma("sync", lambda eng: eng.dma_start(
            out=dbgo[:, :], in_=oh_i[:, :]), deps=[tsi], ch="dbg_s")

    def arms_for(make_dst_src):
        arms = []
        for e in range(8):
            flag = oh_i[0:1, e:e + 1]
            dst, srcv = make_dst_src(e)
            arms.append((flag, lambda eng, dst=dst, srcv=srcv:
                         eng.dma_start(out=dst, in_=srcv)))
        return arms

    tw1l = []
    for kc in range(8):
        tw1l.append(p.route("gpsimd", arms_for(lambda e, kc=kc: (
            w1s[:, kc * 2048:(kc + 1) * 2048],
            hb["L0_ffn_W1"][e][kc * 128:(kc + 1) * 128, :])),
            deps=[tsi], ch=pref + f"_w1g{kc}"))
    tw1 = tw1l[-1]
    tw2l = []
    for kc in range(16):
        tw2l.append(p.route("gpsimd", arms_for(lambda e, kc=kc: (
            w2s[:, kc * 1024:(kc + 1) * 1024],
            hb["L0_ffn_W2"][e][kc * 128:(kc + 1) * 128, :])),
            deps=[tsi], ch=pref + f"_w2g{kc}"))
    tw2 = tw2l[-1]
    def _b1_arm(e):
        dst = b1s[:, :]
        srcv = hb["L0_ffn_b1"][e].rearrange("(k p) -> p k", p=128)
        return dst, srcv
    arms = []
    for e in range(8):
        flag = oh_i[0:1, e:e + 1]
        dst, srcv = _b1_arm(e)
        arms.append((flag, lambda eng, dst=dst, srcv=srcv:
                     eng.dma_start(out=dst, in_=srcv,
                                   allow_slow_non_contiguous=True)))
    tb1g = p.route("gpsimd", arms, deps=[tsi], ch=pref + "_b1g")
    b2r = ctx.sb(pref + "_b2r", [1, DM], F32)
    b2r16 = ctx.sb(pref + "_b2r16", [1, DM], BF16)
    tb2g = p.route("gpsimd", arms_for(lambda e: (
        b2r[0:1, :],
        hb["L0_ffn_b2"][e].rearrange("(a c) -> a c", a=1))),
        deps=[tsi], ch=pref + "_b2g")
    two = []
    for kc in range(8):
        two.append(p.dma("gpsimd", lambda eng, kc=kc: eng.dma_start(
            out=wos[:, kc * DM:(kc + 1) * DM],
            in_=hb["L0f_wo"][kc * 128:(kc + 1) * 128, :]), ch=pref + "_wo"))
    tb2c = p.op("vector", lambda eng: eng.tensor_copy(
        b2r16[0:1, :], b2r[0:1, :]), deps=[tb2g], flush=True)
    tb2t = []
    b2c = ctx.sb(pref + "_b2c", [128, 8], BF16)
    for kc in range(8):
        tb2t.append(tr128(ctx, b2c[:, kc:kc + 1],
                          b2r16[0:1, kc * 128:(kc + 1) * 128],
                          consts, deps=[tb2c], dtype=BF16,
                          evac_engine="vector"))
    bank, bdeps = pool.acquire(F32)
    bank2, bdeps2 = pool.acquire(F32)
    tbe = []
    for half, bk, bd in ((0, bank, bdeps), (1, bank2, bdeps2)):
        t = mm_accum(ctx, bk["t"][0:1, :],
                     [(b2c[:, kc:kc + 1],
                       wos[:, kc * DM + half * 512: kc * DM + half * 512 + 512])
                      for kc in range(8)],
                     deps=tb2t + two, bank_deps=bd)
        tbe.append((bk, t))
    tbo = p.dma("sync", lambda eng: eng.dma_start(
        out=beff[0:1, :], in_=hb["L0f_bo"][:, :]), ch=pref + "_bo")
    beff_toks = []
    for half, (bk, t) in enumerate(tbe):
        tv = p.op("vector", lambda eng, bk=bk, half=half: eng.tensor_add(
            beff[0:1, half * 512:(half + 1) * 512], bk["t"][0:1, :],
            beff[0:1, half * 512:(half + 1) * 512]),
            deps=[t, tbo], flush=True)
        pool.release(bk, tv)
        beff_toks.append(tv)
    tbt = []
    b16 = ctx.sb(pref + "_b16", [1, DM], BF16)
    tb16 = p.op("vector", lambda eng: eng.tensor_copy(
        b16[0:1, :], beff[0:1, :]), deps=beff_toks, flush=True)
    bco = []
    for half in range(2):
        bank, bdeps = pool.acquire(F32)
        tm = mm_accum(ctx, bank["t"][:, :],
                      [(consts["onesbf"][0:1, 0:128],
                        b16[0:1, half * 512:(half + 1) * 512])],
                      deps=[tb16], bank_deps=bdeps)
        tv = p.op("vector", lambda eng, bank=bank, half=half:
                  eng.tensor_scalar_mul(
                      beff[:, half * 512:(half + 1) * 512],
                      bank["t"][:, :], 0.3), deps=[tm], flush=True)
        pool.release(bank, tv)
        bco.append(tv)

    osb = ctx.sb(pref + "_osb", [128, DM], F32)
    xbo = xbz
    prev_ost = None
    for w in range(NW):
        t0 = w * 4
        h_toks = []
        for oc in range(16):
            bank, bdeps = pool.acquire(F32)
            pairs = [(w1s[:, kc * 2048 + oc * 128: kc * 2048 + (oc + 1) * 128],
                      zfm[:, kc * S + t0 * 128: kc * S + (t0 + 4) * 128])
                     for kc in range(8)]
            tmm = mm_accum(ctx, bank["t"][:, :], pairs,
                           deps=ztr + tw1l, bank_deps=bdeps)
            tev = p.op("scalar", lambda eng, bank=bank, oc=oc: eng.activation(
                hfm[:, oc * TW:(oc + 1) * TW], bank["t"][:, :], AF.Gelu,
                bias=b1s[:, oc:oc + 1]), deps=[tmm, tb1g])
            pool.release(bank, tev)
            h_toks.append(tev)
        r_toks = []
        for oc in range(8):
            bank, bdeps = pool.acquire(F32)
            pairs = [(w2s[:, kc * 1024 + oc * 128: kc * 1024 + (oc + 1) * 128],
                      hfm[:, kc * TW:(kc + 1) * TW]) for kc in range(16)]
            tmm = mm_accum(ctx, bank["t"][:, :], pairs,
                           deps=h_toks + tw2l, bank_deps=bdeps)
            tev = p.op("scalar", lambda eng, bank=bank, oc=oc: eng.copy(
                rfm[:, oc * TW:(oc + 1) * TW], bank["t"][:, :]),
                deps=[tmm])
            pool.release(bank, tev)
            r_toks.append(tev)
        for ti in range(4):
            tc = t0 + ti
            xld = p.dma("gpsimd", lambda eng, tc=tc: eng.dma_start(
                out=xbo[:, :], in_=xio["src"][tc]),
                deps=[ctx.last.get(pref + "_xbo_rd")], ch=pref + "_xo")
            zt2 = ln_apply(ctx, xbo[:, :], zb[:, :], stash, tc,
                           deps=[xld, ctx.last.get(pref + "_zb_rd")])
            ptoks = []
            for osl in range(2):
                bank, bdeps = pool.acquire(F32)
                pairs = [(rfm[:, kc * TW + ti * 128: kc * TW + (ti + 1) * 128],
                          wos[:, kc * DM + osl * 512:
                              kc * DM + osl * 512 + 512])
                         for kc in range(8)]
                tmm = mm_accum(ctx, bank["t"][:, :], pairs,
                               deps=r_toks + two, bank_deps=bdeps)
                o = osb[:, osl * 512:(osl + 1) * 512]
                tv = p.op("vector", lambda eng, bank=bank, o=o, osl=osl:
                          eng.scalar_tensor_tensor(
                              o, bank["t"][:, :], 0.3,
                              zb[:, osl * 512:(osl + 1) * 512],
                              ALU.mult, ALU.add),
                          deps=[tmm, zt2, prev_ost])
                pool.release(bank, tv)
                tv2 = p.op("vector", lambda eng, o=o, osl=osl: eng.tensor_add(
                    o, o, xbo[:, osl * 512:(osl + 1) * 512]), deps=[tv])
                tv3 = p.op("vector", lambda eng, o=o, osl=osl: eng.tensor_add(
                    o, o, beff[:, osl * 512:(osl + 1) * 512]),
                    deps=[tv2] + bco)
                ptoks.append(tv3)
            ctx.last[pref + "_xbo_rd"] = ptoks[-1]
            ctx.last[pref + "_zb_rd"] = ptoks[-1]
            prev_ost = p.dma("sync", lambda eng, tc=tc: eng.dma_start(
                out=xio["dst"][tc], in_=osb[:, :]),
                deps=ptoks, ch=pref + "_xso")
    return


BLOCK_EMITTERS["l0_ffn"] = emit_l0_ffn


def prep_l0_ffn(inputs):
    specs = {
        "L0_ffn_W1": ((8, 1024, 2048), F32),
        "L0_ffn_W2": ((8, 2048, 1024), F32),
        "L0_ffn_b1": ((8, 2048), F32),
        "L0_ffn_b2": ((8, 1024), F32),
        "L0f_wp1": ((8, 128, 128), F32),
        "L0f_bp1": ((1, 128), F32),
        "L0f_wp2": ((128, 8), F32),
        "L0f_bp2": ((1, 8), F32),
        "L0f_wo": ((1024, 1024), F32),
        "L0f_bo": ((1, 1024), F32),
    }
    specs.update(COMMON_SPECS)
    arrs = {
        "L0_ffn_W1": np.ascontiguousarray(np.asarray(inputs["L0_ffn_W1"],
                                                     np.float32)),
        "L0_ffn_W2": np.ascontiguousarray(np.asarray(inputs["L0_ffn_W2"],
                                                     np.float32)),
        "L0_ffn_b1": np.ascontiguousarray(np.asarray(inputs["L0_ffn_b1"],
                                                     np.float32)),
        "L0_ffn_b2": np.ascontiguousarray(np.asarray(inputs["L0_ffn_b2"],
                                                     np.float32)),
        "L0f_wp1": np.ascontiguousarray(np.asarray(
            inputs["L0_ffn_Wp1"], np.float32).reshape(8, 128, 128)),
        "L0f_bp1": np.ascontiguousarray(np.asarray(
            inputs["L0_ffn_bp1"], np.float32).reshape(1, 128)),
        "L0f_wp2": np.ascontiguousarray(np.asarray(inputs["L0_ffn_Wp2"],
                                                   np.float32)),
        "L0f_bp2": np.ascontiguousarray(np.asarray(
            inputs["L0_ffn_bp2"], np.float32).reshape(1, 8)),
        "L0f_wo": np.ascontiguousarray(np.asarray(inputs["L0_ffn_Wo"],
                                                  np.float32)),
        "L0f_bo": np.ascontiguousarray(np.asarray(
            inputs["L0_ffn_bo"], np.float32).reshape(1, 1024)),
    }
    arrs.update(common_arrays())
    return specs, arrs





def emit_l3_attn(ctx, xio, hb, consts):
    import os
    npr = int(os.environ.get("L3A_PAIRS", "8"))
    nck = int(os.environ.get("L3A_CHUNKS", "32"))
    skp = int(os.environ.get("L3A_SKIP", "0"))
    nc, p = ctx.nc, ctx.p
    pref = "l3a"
    pool = ctx.pool

    def tensor_now():
        return ("ct", p.cnt["ct"])

    zfm = ctx.sb(pref + "_zfm", [128, 8 * S], BF16)
    yfm = ctx.sb(pref + "_yfm", [128, 8 * S], BF16)
    stash = ctx.sb(pref + "_stash", [128, 64], F32)
    nmb = ctx.sb(pref + "_nmb", [128, 32], F32)
    wpk = ctx.sb(pref + "_wpk", [128, 8 * 388], BF16)
    qtok = ctx.sb(pref + "_qtok", [128, 32 * 128], BF16)
    ktok = ctx.sb(pref + "_ktok", [128, 32 * 128], BF16)
    vaug = [ctx.sb(pref + f"_va{h}", [128, 32 * 65], BF16) for h in range(2)]
    rtm = ctx.sb(pref + "_rtm", [128, 8 * 128], BF16)
    rt2 = ctx.sb(pref + "_rt2", [128, 8 * 128], BF16)
    ccb = ctx.sb(pref + "_cc", [128, 8 * 64], BF16)
    ssb = ctx.sb(pref + "_ss", [128, 8 * 64], BF16)
    bvals = ctx.sb(pref + "_bv", [128, 4 * 6], F32)
    bb16 = ctx.sb(pref + "_b16", [128, 8], BF16)
    utb = ctx.sb(pref + "_ut", [128, 128], BF16)
    allb = ctx.sb(pref + "_ab", [128, 128], BF16)
    expt = ctx.sb(pref + "_expt", [128, 32 * 8], F32)
    gend = ctx.sb(pref + "_gend", [128, 32 * 4], F32)
    qs2 = ctx.sb(pref + "_qs2", [128, 256], BF16)
    ks2 = ctx.sb(pref + "_ks2", [128, 256], BF16)
    qfm2 = ctx.sb(pref + "_qfm2", [128, 256], BF16)
    kfm2 = ctx.sb(pref + "_kfm2", [128, 256], BF16)
    ptl = ctx.sb(pref + "_ptl", [128, 128], BF16)
    ytk = ctx.sb(pref + "_ytk", [128, 128], BF16)
    Mbuf = ctx.sb(pref + "_M", [128, 16 * 65], F32)
    Mb16 = ctx.sb(pref + "_M16", [128, 2 * 65], BF16)
    nsb = ctx.sb(pref + "_nsb", [128, 128], F32)
    rrow = ctx.sb(pref + "_rr", [1, 256], F32)
    dmsk = ctx.sb(pref + "_dm", [128, 128], BF16)
    bbbc = ctx.sb(pref + "_bb", [128, 32], F32)
    ones = consts["ones_f32"]

    tdm = p.dma("gpsimd", lambda eng: eng.dma_start(
        out=dmsk[:, :], in_=hb["L3a_dmask"][:, :]), ch=pref + "_dm")
    tbb = p.dma("sync", lambda eng: eng.dma_start(
        out=bbbc[:, :], in_=hb["L3a_bb"][:, :]), ch=pref + "_bb")
    tut = p.dma("gpsimd", lambda eng: eng.dma_start(
        out=utb[:, :], in_=hb["L3a_ut"][:, :]), ch=pref + "_ut")
    tab = p.op("vector", lambda eng: eng.memset(allb[:, :], 1.0), deps=[],
               flush=True)
    tmzy = p.op("vector", lambda eng: eng.memset(ytk[:, :], 0.0), deps=[],
                flush=True)

    tstats = emit_ln_stats(ctx, xio, pref, stash[:, :], nmb)
    xbz = ctx.sb(pref + "_sxb0", [128, DM], F32)
    zb = ctx.sb(pref + "_sxb1", [128, DM], F32)
    ztile = ctx.sb(pref + "_zt", [128, DM], BF16)
    prev_zrd = None
    ztr = []
    for tc in range(NT):
        xld = p.dma("gpsimd", lambda eng, tc=tc: eng.dma_start(
            out=xbz[:, :], in_=xio["src"][tc]),
            deps=[prev_zrd, ctx.last.get(pref + "_zap"), tstats],
            ch=pref + "_xz")
        zt = ln_apply(ctx, xbz[:, :], ztile[:, :], stash, tc,
                      deps=[xld, tstats, prev_zrd])
        ctx.last[pref + "_zap"] = zt
        for kc in range(8):
            ztr.append(tr128(
                ctx, zfm[:, kc * S + tc * 128: kc * S + (tc + 1) * 128],
                ztile[:, kc * 128:(kc + 1) * 128], consts, deps=[zt]))
        prev_zrd = tensor_now()

    vone = []
    for h in range(2):
        ap = vaug[h][:, :].rearrange("p (n c) -> p n c", c=65)[:, :, 64:65]
        vone.append(p.op("vector", lambda eng, ap=ap: eng.memset(ap, 1.0),
                         deps=[], flush=True))

    for hp in range(npr):
        twp = p.dma("gpsimd", lambda eng, hp=hp: eng.dma_start(
            out=wpk[:, :].rearrange("p (k c) -> p k c", c=388),
            in_=hb["L3a_wpk"][hp].rearrange("(k p) c -> p k c", p=128)),
            deps=[ctx.last.get(pref + "_wpk_rd")], ch=pref + "_wpk")
        pj_toks = []
        bl_toks = {}
        for cn in range(NT):
            bank, bdeps = pool.acquire(F32)
            pairs = [(zfm[:, kc * S + cn * 128:kc * S + (cn + 1) * 128],
                      wpk[:, kc * 388:(kc + 1) * 388]) for kc in range(8)]
            tmm = mm_accum(ctx, bank["t"][:, 0:388], pairs,
                           deps=ztr + [twp], bank_deps=bdeps)
            tq = p.op("scalar", lambda eng, bank=bank, cn=cn: eng.copy(
                qtok[:, cn * 128:(cn + 1) * 128], bank["t"][:, 0:128]),
                deps=[tmm, ctx.last.get(pref + "_qtok_rd")])
            tk = p.op("scalar", lambda eng, bank=bank, cn=cn: eng.copy(
                ktok[:, cn * 128:(cn + 1) * 128], bank["t"][:, 128:256]),
                deps=[tmm, ctx.last.get(pref + "_ktok_rd")])
            tvs = []
            for h in range(2):
                tvs.append(p.op("scalar", lambda eng, bank=bank, cn=cn, h=h:
                           eng.copy(vaug[h][:, cn * 65:cn * 65 + 64],
                                    bank["t"][:, 256 + h * 64:
                                              256 + (h + 1) * 64]),
                           deps=[tmm, vone[h],
                                 ctx.last.get(pref + f"_va{h}_rd")]))
            if skp & 8:
                pool.release(bank, [tq, tk] + tvs)
                pj_toks += [tq, tk] + tvs
                bl_toks[cn] = dict(q=tq, k=tk, v=tvs, e=[], g=None)
                continue
            tbe = p.op("scalar", lambda eng, bank=bank: eng.copy(
                bvals[:, 0:4], bank["t"][:, 384:388]),
                deps=[tmm, ctx.last.get(pref + "_bv_rd"),
                      ctx.last.get(pref + "_bv_sc")], flush=True)
            tb0 = p.op("vector", lambda eng: eng.tensor_add(
                bvals[:, 0:4], bvals[:, 0:4],
                bbbc[:, hp * 4:hp * 4 + 4]),
                deps=[tbe, tbb], flush=True)
            tb1 = p.op("scalar", lambda eng: eng.activation(
                bvals[:, 4:8], bvals[:, 0:4], AF.Exp, scale=-1.0),
                deps=[tb0], flush=True)
            tb2 = p.op("scalar", lambda eng: eng.activation(
                bvals[:, 8:12], bvals[:, 4:8], AF.Ln, bias=1.0),
                deps=[tb1], flush=True)
            tb3 = p.op("vector", lambda eng: eng.tensor_scalar_mul(
                bvals[:, 12:16], bvals[:, 8:12], -1.0), deps=[tb2])
            tb4 = p.op("vector", lambda eng: eng.tensor_scalar_max(
                bvals[:, 12:16], bvals[:, 12:16], -6.907755279),
                deps=[tb3])
            tb5 = p.op("vector", lambda eng: eng.tensor_scalar_mul(
                bvals[:, 16:20], bvals[:, 12:16], -1.0), deps=[tb4])
            tb6 = p.op("vector", lambda eng: eng.tensor_scalar_max(
                bvals[:, 16:20], bvals[:, 16:20], 0.0010005003),
                deps=[tb5])
            tb7 = p.op("vector", lambda eng: eng.tensor_scalar_mul(
                bvals[:, 16:20], bvals[:, 16:20], -1.0), deps=[tb6],
                flush=True)
            th1 = p.dma("gpsimd", lambda eng: eng.dma_start(
                out=bb16[:, 0:4], in_=bvals[:, 16:20],
                allow_slow_non_contiguous=True),
                deps=[tb7, ctx.last.get(pref + "_b16_rd")], ch=pref + "_bc1")
            th2 = p.op("scalar", lambda eng: eng.copy(
                bvals[:, 12:16], bb16[:, 0:4]), deps=[th1], flush=True)
            th3 = p.op("vector", lambda eng: eng.tensor_sub(
                bvals[:, 20:24], bvals[:, 16:20], bvals[:, 12:16]),
                deps=[th2], flush=True)
            th4 = p.dma("gpsimd", lambda eng: eng.dma_start(
                out=bb16[:, 4:8], in_=bvals[:, 20:24],
                allow_slow_non_contiguous=True),
                deps=[th3, ctx.last.get(pref + "_b16_rd")], ch=pref + "_bc2")
            ctx.last[pref + "_bv_sc"] = th2
            if skp & 16:
                pool.release(bank, [tq, tk] + tvs + [tbe])
                pj_toks += [tq, tk] + tvs
                bl_toks[cn] = dict(q=tq, k=tk, v=tvs, e=[], g=None)
                continue
            bkc, bkcd = pool.acquire(F32)
            tcs = mm_accum(ctx, bkc["t"][:, 0:4],
                           [(utb[:, :], bb16[:, 0:4]),
                            (utb[:, :], bb16[:, 4:8])],
                           deps=[th1, th4, tut], bank_deps=bkcd)
            bkg, bkgd = pool.acquire(F32)
            tLe = mm_accum(ctx, bkg["t"][:, 0:4],
                           [(allb[:, :], bb16[:, 0:4]),
                            (allb[:, :], bb16[:, 4:8])],
                           deps=[th1, th4, tab], bank_deps=bkgd)
            ctx.last[pref + "_b16_rd"] = tensor_now()
            t_ep = p.op("scalar", lambda eng, bkc=bkc, cn=cn: eng.activation(
                expt[:, cn * 8:cn * 8 + 4], bkc["t"][:, 0:4], AF.Exp),
                deps=[tcs, ctx.last.get(pref + "_expt_rd")], flush=True)
            t_en = p.op("scalar", lambda eng, bkc=bkc, cn=cn: eng.activation(
                expt[:, cn * 8 + 4:cn * 8 + 8], bkc["t"][:, 0:4], AF.Exp,
                scale=-1.0), deps=[tcs], flush=True)
            pool.release(bkc, [t_ep, t_en])
            t_g = p.op("scalar", lambda eng, bkg=bkg, cn=cn: eng.activation(
                gend[:, cn * 4:(cn + 1) * 4], bkg["t"][:, 0:4], AF.Exp),
                deps=[tLe, ctx.last.get(pref + "_gend_rd")], flush=True)
            pool.release(bkg, [t_g])
            bl_toks[cn] = dict(q=tq, k=tk, v=tvs, e=[t_ep, t_en], g=t_g)
            pool.release(bank, [tq, tk] + tvs + [tbe])
            pj_toks += [tq, tk] + tvs


        ctx.last[pref + "_wpk_rd"] = tensor_now()
        rp_toks = []
        for g in (range(4) if not (skp & 1) else []):
            g0 = g * 8
            tcc = p.dma("gpsimd", lambda eng, g0=g0: eng.dma_start(
                out=ccb[:, :], in_=hb["L3a_cc"][:, g0 * 64:(g0 + 8) * 64]),
                deps=[ctx.last.get(pref + "_cc_rd")], ch=pref + "_cc")
            tss = p.dma("gpsimd", lambda eng, g0=g0: eng.dma_start(
                out=ssb[:, :], in_=hb["L3a_ss"][:, g0 * 64:(g0 + 8) * 64]),
                deps=[ctx.last.get(pref + "_ss_rd")], ch=pref + "_ss")
            deps_qk = [bl_toks[c2][m] for c2 in range(g0, g0 + 8)
                       for m in ("q", "k")]

            for mat, mkey in ((qtok, "qr"), (ktok, "kr")):
                m3 = mat[:, g0 * 128:(g0 + 8) * 128].rearrange(
                    "p (n c) -> p n c", c=128)
                c3 = ccb[:, :].rearrange("p (n c) -> p n c", c=64)
                s3 = ssb[:, :].rearrange("p (n c) -> p n c", c=64)
                r3 = rtm[:, :].rearrange("p (n c) -> p n c", c=128)
                r23 = rt2[:, :].rearrange("p (n c) -> p n c", c=128)
                t6 = None
                for hh in range(2):
                    h0 = m3[:, :, hh * 64:hh * 64 + 32]
                    h1 = m3[:, :, hh * 64 + 32:hh * 64 + 64]
                    ccv = c3[:, :, 0:32]
                    ssv = s3[:, :, 0:32]
                    rv = r3[:, :, hh * 32:hh * 32 + 32]
                    r2v = r23[:, :, hh * 32:hh * 32 + 32]
                    t1 = p.op("vector", lambda eng, h0=h0, ccv=ccv, rv=rv:
                              eng.tensor_mul(rv, h0, ccv),
                              deps=deps_qk + [tcc, t6,
                                              ctx.last.get(pref + "_rtm_rd")])
                    t2 = p.op("vector", lambda eng, h1=h1, ssv=ssv, r2v=r2v:
                              eng.tensor_mul(r2v, h1, ssv),
                              deps=deps_qk + [tss, t6,
                                              ctx.last.get(pref + "_rt2_rd")])
                    t3 = p.op("vector", lambda eng, h1=h1, ccv=ccv:
                              eng.tensor_mul(h1, h1, ccv), deps=[t2, tcc])
                    t4 = p.op("vector", lambda eng, h0=h0, ssv=ssv:
                              eng.tensor_mul(h0, h0, ssv), deps=[t1, t3, tss])
                    t5 = p.op("vector", lambda eng, h0=h0, h1=h1:
                              eng.tensor_add(h1, h1, h0), deps=[t4])
                    t6 = p.op("vector", lambda eng, h0=h0, rv=rv, r2v=r2v:
                              eng.tensor_sub(h0, rv, r2v), deps=[t5])
                gsl = mat[:, g0 * 128:(g0 + 8) * 128]
                t7 = p.op("scalar", lambda eng, gsl=gsl: eng.activation(
                    rtm[:, :], gsl, AF.Relu, scale=-1.0), deps=[t6])
                t8 = p.op("scalar", lambda eng: eng.activation(
                    rtm[:, :], rtm[:, :], AF.Exp, scale=-1.0), deps=[t7])
                t9 = p.op("vector", lambda eng, gsl=gsl: eng.tensor_scalar_max(
                    gsl, gsl, 0.0), deps=[t8])
                ta = p.op("vector", lambda eng, gsl=gsl: eng.tensor_add(
                    gsl, gsl, rtm[:, :]), deps=[t9], flush=True)
                ctx.last[pref + "_rtm_rd"] = ta
                ctx.last[pref + "_rt2_rd"] = ta
                rp_toks.append(ta)
            ctx.last[pref + "_cc_rd"] = rp_toks[-1]
            ctx.last[pref + "_ss_rd"] = rp_toks[-1]

        if hp == 0:
            tmz = p.op("vector", lambda eng: eng.memset(Mbuf[:, :], 0.0),
                       deps=[], flush=True)
            ctx.last[pref + "_M_clr"] = tmz

        prev_upd = {0: None, 1: None}
        for cn in range(NT if not (skp & 2) else 0)[:nck]:
            bl = bl_toks[cn]
            sc_toks = []
            for h in range(2):
                for c in range(2):
                    tq_ = p.op("vector", lambda eng, cn=cn, h=h, c=c:
                               eng.tensor_scalar(
                                   qs2[:, (h * 2 + c) * 64:
                                       (h * 2 + c) * 64 + 64],
                                   qtok[:, cn * 128 + h * 64:
                                        cn * 128 + h * 64 + 64],
                                   expt[:, cn * 8 + h * 2 + c:
                                        cn * 8 + h * 2 + c + 1], None,
                                   ALU.mult),
                               deps=rp_toks + bl["e"] +
                               [ctx.last.get(pref + "_qs2_rd")])
                    tk_ = p.op("vector", lambda eng, cn=cn, h=h, c=c:
                               eng.tensor_scalar(
                                   ks2[:, (h * 2 + c) * 64:
                                       (h * 2 + c) * 64 + 64],
                                   ktok[:, cn * 128 + h * 64:
                                        cn * 128 + h * 64 + 64],
                                   expt[:, cn * 8 + 4 + h * 2 + c:
                                        cn * 8 + 4 + h * 2 + c + 1],
                                   None, ALU.mult),
                               deps=rp_toks + bl["e"] +
                               [ctx.last.get(pref + "_ks2_rd")])
                    sc_toks += [tq_, tk_]
            tr_toks = []
            for h in range(2):
                tr_toks.append(tr128(
                    ctx, qfm2[:, h * 128:(h + 1) * 128],
                    qs2[:, h * 128:(h + 1) * 128], consts,
                    deps=sc_toks + [ctx.last.get(pref + "_qfm2_rd")]))
                tr_toks.append(tr128(
                    ctx, kfm2[:, h * 128:(h + 1) * 128],
                    ks2[:, h * 128:(h + 1) * 128], consts,
                    deps=sc_toks + [ctx.last.get(pref + "_kfm2_rd")]))
            ctx.last[pref + "_qs2_rd"] = tensor_now()
            ctx.last[pref + "_ks2_rd"] = tensor_now()
            for h in range(2):
                hcol = (2 * hp + h) * 65
                scb, scdeps = pool.acquire(F32)
                tsc1 = mm_accum(
                    ctx, scb["t"][:, 0:128],
                    [(kfm2[:, h * 128:(h + 1) * 128],
                      qfm2[:, h * 128:(h + 1) * 128])],
                    deps=tr_toks, bank_deps=scdeps)
                tmk = p.op("vector", lambda eng, scb=scb: eng.tensor_mul(
                    ptl[:, :], scb["t"][:, 0:128], dmsk[:, :]),
                    deps=[tsc1, tdm, ctx.last.get(pref + "_ptl_rd")])
                pool.release(scb, tmk)
                yb, ybd = pool.acquire(F32)
                tym = mm_accum(
                    ctx, yb["t"][:, 0:65],
                    [(ptl[:, :], vaug[h][:, cn * 65:(cn + 1) * 65])],
                    deps=[tmk] + bl["v"], bank_deps=ybd,
                    start=True, stop=False)
                ctx.last[pref + "_ptl_rd"] = tym
                tmc = p.op("vector", lambda eng, h=h, hcol=hcol:
                           eng.tensor_copy(Mb16[:, h * 65:(h + 1) * 65],
                                           Mbuf[:, hcol:hcol + 65]),
                           deps=[prev_upd[h],
                                 ctx.last.get(pref + "_M_clr"),
                                 ctx.last.get(pref + f"_M16{h}_rd")],
                           flush=True)
                tyi = mm_accum(
                    ctx, yb["t"][:, 0:65],
                    [(qfm2[:, h * 128:(h + 1) * 128],
                      Mb16[:, h * 65:(h + 1) * 65])],
                    deps=[tmc],
                    bank_deps=[], start=False, stop=True)
                ctx.last[pref + f"_M16{h}_rd"] = tyi
                tev = p.op("scalar", lambda eng, yb=yb: eng.copy(
                    nsb[:, 0:65], yb["t"][:, 0:65]),
                    deps=[tyi, ctx.last.get(pref + "_nsb_rd")])
                pool.release(yb, tev)
                tad = p.op("vector", lambda eng: eng.tensor_scalar_add(
                    nsb[:, 66:67], nsb[:, 64:65], 1e-6),
                    deps=[tev], flush=True)
                trc = p.op("vector", lambda eng: eng.reciprocal(
                    nsb[:, 67:68], nsb[:, 66:67]), deps=[tad],
                    flush=True)
                tsy = p.op("vector", lambda eng, h=h: eng.tensor_scalar(
                    ytk[:, h * 64:(h + 1) * 64], nsb[:, 0:64],
                    nsb[:, 67:68], None, ALU.mult),
                    deps=[trc, ctx.last.get(pref + "_ytk_rd")], flush=True)
                ctx.last[pref + "_nsb_rd"] = tsy
                trb, trbd = pool.acquire(BF16)
                ttr = p.op("tensor", lambda eng, trb=trb: eng.transpose(
                    trb["t"][0:128, 0:128], ytk[:, :], consts["ident_bf"]),
                    deps=[tsy, tmzy] + trbd)
                dst = yfm[h * 64:(h + 1) * 64,
                          hp * S + cn * 128: hp * S + (cn + 1) * 128]
                tmu = p.op("vector", lambda eng, dst=dst, trb=trb, h=h:
                           eng.tensor_copy(
                               dst, trb["t"][h * 64:(h + 1) * 64, 0:128]),
                           deps=[ttr])
                pool.release(trb, tmu)
                ctx.last[pref + "_ytk_rd"] = ("ct", p.cnt["ct"])
                sb2, sbd = pool.acquire(F32)
                tsm = mm_accum(
                    ctx, sb2["t"][:, 0:65],
                    [(ks2[:, h * 128:(h + 1) * 128],
                      vaug[h][:, cn * 65:(cn + 1) * 65])],
                    deps=sc_toks + bl["v"], bank_deps=sbd)
                tup = p.op("vector", lambda eng, sb2=sb2, hcol=hcol:
                           eng.tensor_add(
                               Mbuf[:, hcol:hcol + 65],
                               sb2["t"][:, 0:65],
                               Mbuf[:, hcol:hcol + 65]),
                           deps=[tsm, tyi, prev_upd[h],
                                 ctx.last.get(pref + "_M_clr")])
                pool.release(sb2, tup)
                for c in range(2):
                    tsc2 = p.op("vector", lambda eng, c=c, hcol=hcol, h=h:
                                eng.tensor_scalar(
                                    Mbuf[c * 64:(c + 1) * 64,
                                         hcol:hcol + 65],
                                    Mbuf[c * 64:(c + 1) * 64,
                                         hcol:hcol + 65],
                                    gend[c * 64:c * 64 + 64,
                                         cn * 4 + h * 2 + c:
                                         cn * 4 + h * 2 + c + 1],
                                    None, ALU.mult),
                                deps=[tup, bl["g"],
                                      (prev_upd[h] if c else None)],
                                flush=True)
                    prev_upd[h] = tsc2
            ctx.last[pref + "_qfm2_rd"] = tensor_now()
            ctx.last[pref + "_kfm2_rd"] = tensor_now()
        ctx.last[pref + "_qtok_rd"] = ("cv", p.cnt["cv"])
        ctx.last[pref + "_ktok_rd"] = ("cv", p.cnt["cv"])
        ctx.last[pref + "_bv_rd"] = ("cv", p.cnt["cv"])
        ctx.last[pref + "_expt_rd"] = ("cv", p.cnt["cv"])
        ctx.last[pref + "_gend_rd"] = ("cv", p.cnt["cv"])
        for h in range(2):
            ctx.last[pref + f"_va{h}_rd"] = tensor_now()
        ctx.last[pref + "_M_rd"] = tensor_now()
    y_done = ("cv", p.cnt["cv"])
    y_done_s = ("cs", p.cnt["cs"])
    qk_free = tensor_now()

    hh_d = nc.dram_tensor("l3a_hh", [S, DM], F32, kind="Internal").ap()
    hh_v = hh_d.rearrange("(n p) d -> n p d", p=128)
    wo_sb = qtok
    wo_sb2 = ktok
    wo_toks = []
    for kc in range(8):
        dst = wo_sb if kc < 4 else wo_sb2
        off = (kc % 4) * DM
        wo_toks.append(p.dma("gpsimd", lambda eng, kc=kc, dst=dst, off=off:
                             eng.dma_start(
                                 out=dst[:, off:off + DM],
                                 in_=hb["L3a_wo"][kc * 128:(kc + 1) * 128, :]),
                             deps=[qk_free, y_done, y_done_s],
                             ch=pref + "_wo"))

    def wo_ap(kc, osl):
        dst = wo_sb if kc < 4 else wo_sb2
        off = (kc % 4) * DM
        return dst[:, off + osl * 512: off + osl * 512 + 512]

    bob = ctx.sb(pref + "_bob", [128, DM], F32)
    tbo = p.dma("sync", lambda eng: eng.dma_start(
        out=bob[:, :], in_=hb["L3a_bo"][:, :]), ch=pref + "_bo")
    hsb = ctx.sb(pref + "_hsb", [128, DM], F32)
    prev_hst = None
    for tc in range(NT if not (skp & 4) else 0):
        xld = p.dma("gpsimd", lambda eng, tc=tc: eng.dma_start(
            out=xbz[:, :], in_=xio["src"][tc]),
            deps=[ctx.last.get(pref + "_xbz2_rd")], ch=pref + "_xz")
        zt = ln_apply(ctx, xbz[:, :], zb[:, :], stash, tc,
                      deps=[xld, ctx.last.get(pref + "_zb2_rd")])
        ctx.last[pref + "_xbz2_rd"] = zt
        ptoks = []
        for osl in range(2):
            bank, bdeps = pool.acquire(F32)
            pairs = [(yfm[:, kc * S + tc * 128: kc * S + (tc + 1) * 128],
                      wo_ap(kc, osl)) for kc in range(8)]
            tmm = mm_accum(ctx, bank["t"][:, :], pairs,
                           deps=[y_done] + wo_toks, bank_deps=bdeps)
            o = hsb[:, osl * 512:(osl + 1) * 512]
            tv = p.op("vector", lambda eng, bank=bank, o=o, osl=osl:
                      eng.tensor_add(o, bank["t"][:, :],
                                     zb[:, osl * 512:(osl + 1) * 512]),
                      deps=[tmm, zt, prev_hst])
            pool.release(bank, tv)
            tv2 = p.op("vector", lambda eng, o=o, osl=osl: eng.tensor_add(
                o, o, bob[:, osl * 512:(osl + 1) * 512]), deps=[tv, tbo])
            ptoks.append(tv2)
        ctx.last[pref + "_zb2_rd"] = ptoks[-1]
        prev_hst = p.dma("sync", lambda eng, tc=tc: eng.dma_start(
            out=hh_v[tc], in_=hsb[:, :]), deps=ptoks, ch=pref + "_hs")
    stash2 = ctx.sb(pref + "_stash2", [128, 64], F32)
    nmb2 = ctx.sb(pref + "_nmb2", [128, 32], F32)
    hh_xio = {"src": hh_v}
    tst2 = emit_ln_stats(ctx, hh_xio, pref, stash2[:, :], nmb2,
                         deps0=[prev_hst,
                                ctx.last.get(pref + "_xbz2_rd"),
                                ctx.last.get(pref + "_zb2_rd")])
    passA_done = ("cv", p.cnt["cv"])
    gb = bob
    bb2 = Mbuf[:, 0:DM]
    tg = p.dma("sync", lambda eng: eng.dma_start(
        out=gb[:, :], in_=hb["L3a_g"][:, :]), deps=[passA_done],
        ch=pref + "_g")
    tb = p.dma("gpsimd", lambda eng: eng.dma_start(
        out=bb2[:, :], in_=hb["L3a_b"][:, :]), deps=[passA_done, qk_free],
        ch=pref + "_b")
    osb = hsb
    prev_ost = None
    for tc in range(NT):
        hld = p.dma("gpsimd", lambda eng, tc=tc: eng.dma_start(
            out=zb[:, :], in_=hh_v[tc]),
            deps=[prev_hst, ctx.last.get(pref + "_zb3_rd"), tst2],
            ch=pref + "_hl")
        xld = p.dma("sync", lambda eng, tc=tc: eng.dma_start(
            out=xbz[:, :], in_=xio["src"][tc]),
            deps=[ctx.last.get(pref + "_xbz3_rd"), tst2], ch=pref + "_xl3")
        t1 = ln_apply(ctx, zb[:, :], zb[:, :], stash2, tc,
                      deps=[hld, tst2, prev_ost])
        t2 = p.op("vector", lambda eng: eng.tensor_mul(
            zb[:, :], zb[:, :], gb[:, :]), deps=[t1, tg])
        t3 = p.op("vector", lambda eng: eng.tensor_add(
            zb[:, :], zb[:, :], bb2[:, :]), deps=[t2, tb])
        t4 = p.op("vector", lambda eng: eng.tensor_add(
            osb[:, :], zb[:, :], xbz[:, :]), deps=[t3, xld, prev_hst])
        ctx.last[pref + "_zb3_rd"] = t4
        ctx.last[pref + "_xbz3_rd"] = t4
        prev_ost = p.dma("sync", lambda eng, tc=tc: eng.dma_start(
            out=xio["dst"][tc], in_=osb[:, :]), deps=[t4],
            ch=pref + "_xso")
    return


BLOCK_EMITTERS["l3_attn"] = emit_l3_attn


def prep_l3_attn(inputs):
    Wq = np.asarray(inputs["L3_attn_Wq"], np.float32)
    Wk = np.asarray(inputs["L3_attn_Wk"], np.float32)
    Wv = np.asarray(inputs["L3_attn_Wv"], np.float32)
    Wb = np.asarray(inputs["L3_attn_Wbeta"], np.float32)
    wpk = np.zeros((8, 1024, 388), np.float32)
    for hp in range(8):
        wpk[hp, :, 0:128] = Wq[:, hp * 128:(hp + 1) * 128]
        wpk[hp, :, 128:256] = Wk[:, hp * 128:(hp + 1) * 128]
        wpk[hp, :, 256:384] = Wv[:, hp * 128:(hp + 1) * 128]
        wpk[hp, :, 384:388] = Wb[:, hp * 4:(hp + 1) * 4]
    half = 32
    inv_freq = 1.0 / 10000.0 ** (np.arange(0, 64, 2, dtype=np.float32) / 64)
    t = np.arange(S, dtype=np.float32)
    fr = t[:, None] * inv_freq[None, :]
    cos = np.cos(fr).astype(np.float32)
    sin = np.sin(fr).astype(np.float32)
    cc = np.tile(cos.reshape(32, 128, 32), (1, 1, 2)).transpose(1, 0, 2) \
        .reshape(128, 32 * 64)
    ss = np.tile(sin.reshape(32, 128, 32), (1, 1, 2)).transpose(1, 0, 2) \
        .reshape(128, 32 * 64)
    jp = np.arange(128)[:, None]
    tq = np.arange(128)[None, :]
    dmask = (jp <= tq).astype(np.float32)
    specs = {
        "L3a_wpk": ((8, 1024, 388), F32),
        "L3a_cc": ((128, 32 * 64), F32),
        "L3a_ss": ((128, 32 * 64), F32),
        "L3a_bb": ((128, 32), F32),
        "L3a_dmask": ((128, 128), F32),
        "L3a_ut": ((128, 128), F32),
        "L3a_wo": ((1024, 1024), F32),
        "L3a_bo": ((128, 1024), F32),
        "L3a_g": ((128, 1024), F32),
        "L3a_b": ((128, 1024), F32),
    }
    specs.update(COMMON_SPECS)
    arrs = {
        "L3a_wpk": np.ascontiguousarray(wpk),
        "L3a_cc": np.ascontiguousarray(cc),
        "L3a_ss": np.ascontiguousarray(ss),
        "L3a_bb": np.ascontiguousarray(np.broadcast_to(
            np.asarray(inputs["L3_attn_bbeta"], np.float32), (128, 32))),
        "L3a_dmask": dmask,
        "L3a_ut": np.triu(np.ones((128, 128), np.float32)),
        "L3a_wo": np.ascontiguousarray(np.asarray(inputs["L3_attn_Wo"],
                                                  np.float32)),
        "L3a_bo": np.ascontiguousarray(np.broadcast_to(
            np.asarray(inputs["L3_attn_bo"], np.float32), (128, 1024))),
        "L3a_g": np.ascontiguousarray(np.broadcast_to(
            np.asarray(inputs["L3_attn_ln_gamma"], np.float32), (128, 1024))),
        "L3a_b": np.ascontiguousarray(np.broadcast_to(
            np.asarray(inputs["L3_attn_ln_beta"], np.float32), (128, 1024))),
    }
    arrs.update(common_arrays())
    return specs, arrs


BLOCK_ORDER = ["l0_attn", "l0_ffn", "l1_attn", "l1_ffn",
               "l2_attn", "l2_ffn", "l3_attn", "l3_ffn"]

PREPS = {}


def kernel(**inputs):
    x = np.asarray(inputs["L0_attn_x"], np.float32)
    B = x.shape[0]
    cur = [np.ascontiguousarray(x[b]) for b in range(B)]
    for blk in BLOCK_ORDER:
        specs, arrs = PREPS[blk](inputs)
        nc = build_blocks_program([blk], specs)
        in_maps = []
        for b in range(B):
            m = dict(arrs)
            m["x_in"] = cur[b]
            in_maps.append(m)
        res = run_bass_kernel_spmd(nc, in_maps, core_ids=list(range(B)))
        cur = [np.ascontiguousarray(res.results[b]["x_out"])
               for b in range(B)]
    return np.stack(cur, 0)


PREPS.update({n: globals()["prep_" + n] for n in BLOCK_ORDER})
